# version 22
# baseline (speedup 1.0000x reference)
import math
import numpy as np
from contextlib import ExitStack

import concourse.bass as bass
import concourse.mybir as mybir
from concourse.bass_utils import run_bass_kernel_spmd

F32 = mybir.dt.float32
BF16 = mybir.dt.bfloat16
AF = mybir.ActivationFunctionType
ALU = mybir.AluOpType

NCORES = 8
T = 2048
D = 4096
NCH = 32
NH = 32
HD = 128
NEX = 4
LR = 16
EPS = 1e-5
EXP_SCALE = 1.0 / math.sqrt(HD)
EXP_BIAS = -40.0
GATE_BIAS = -10.0

ENGS = ["gpsimd", "sync", "tensor", "scalar", "vector"]


class Track:
    def __init__(self, name):
        self.name = name
        self.n = 0
        self.h = None


class Buf:
    def __init__(self, name, rng=None):
        self.name = name
        self.w = None
        self.rd = {}
        self.dsem = None
        self.rng = rng


class Plan:
    def __init__(self, mk_sem):
        self.ranged = []
        self.ops = {e: [] for e in ENGS}
        self.etr = {e: Track("t_" + e) for e in ENGS}
        self.dsems = []
        self.mk_sem = mk_sem
        self.waited = {e: {} for e in ENGS}

    def all_tracks(self):
        return list(self.etr.values()) + self.dsems

    def get_dsem(self, buf):
        if buf.dsem is None:
            tr = Track("d_" + buf.name)
            tr.h = self.mk_sem(tr.name)
            buf.dsem = tr
            self.dsems.append(tr)
        return buf.dsem

    def _wait(self, eng, tr, n):
        if n <= 0 or self.waited[eng].get(tr.name, 0) >= n:
            return
        self.waited[eng][tr.name] = n
        self.ops[eng].append(lambda e, h=tr, m=n: e.wait_ge(h.h, m))

    def step(self, eng, fns, reads=(), writes=(), dma=False, dsem=None):
        for b in list(reads) + list(writes):
            if b.w is not None:
                self._wait(eng, b.w[0], b.w[1])
            if b.rng is not None:
                for ob in self.ranged:
                    if ob is b or ob.rng[1] <= b.rng[0] or ob.rng[0] >= b.rng[1]:
                        continue
                    if ob.w is not None:
                        self._wait(eng, ob.w[0], ob.w[1])
                    if b in writes:
                        for tr, n in ob.rd.values():
                            self._wait(eng, tr, n)
        for b in writes:
            for tr, n in b.rd.values():
                self._wait(eng, tr, n)
        if not isinstance(fns, (list, tuple)):
            fns = [fns]
        if dma:
            assert dsem is not None, "dma step needs dsem buf"
            tr = self.get_dsem(dsem)
        else:
            tr = self.etr[eng]
        amt = 16 if dma else 1
        tr.n += amt
        fns = list(fns)
        last = fns[-1]
        fns[-1] = lambda e, f=last, h=tr, a=amt: f(e).then_inc(h.h, a)
        self.ops[eng].extend(fns)
        cert = (tr, tr.n)
        for b in writes:
            b.w = cert
            b.rd = {}
        for b in reads:
            b.rd[tr.name] = cert

    def barrier(self):
        snap = [(t, t.n) for t in self.all_tracks()]
        for e in ENGS:
            for t, n in snap:
                if t is self.etr[e]:
                    continue
                self._wait(e, t, n)

    def emit(self, block):
        for name in ENGS:
            fns = self.ops[name]
            if not fns:
                continue

            def body(e, fns=fns):
                for f in fns:
                    f(e)

            getattr(block, name)(body)


def build_nc():
    nc = bass.Bass("TRN2", target_bir_lowering=False)

    xh = nc.dram_tensor("xh", [128, NCH, T], F32, kind="ExternalInput").ap()
    wqkv = nc.dram_tensor("wqkv", [NH, 3, 128, NCH, 128], F32, kind="ExternalInput").ap()
    wo_a = nc.dram_tensor("wo_a", [NCH, 128, NH, 128], F32, kind="ExternalInput").ap()
    a_a = nc.dram_tensor("a_a", [128, NCH, NEX * LR], F32, kind="ExternalInput").ap()
    b_a = nc.dram_tensor("b_a", [NEX * LR, D], F32, kind="ExternalInput").ap()
    wr_a = nc.dram_tensor("wr_a", [128, NCH, NEX], F32, kind="ExternalInput").ap()
    bias_a = nc.dram_tensor("bias_a", [NH, 3 * 128], F32, kind="ExternalInput").ap()
    bo_a = nc.dram_tensor("bo_a", [NCH, 128], F32, kind="ExternalInput").ap()
    eye_a = nc.dram_tensor("eye_a", [128, 128], F32, kind="ExternalInput").ap()
    mpad_a = nc.dram_tensor("mpad_a", [128, 1152], F32, kind="ExternalInput").ap()
    ones_a = nc.dram_tensor("ones_a", [1, 512], F32, kind="ExternalInput").ap()
    e_a = nc.dram_tensor("e_a", [NEX, NEX * LR], F32, kind="ExternalInput").ap()

    out_hbm = nc.dram_tensor("out_hbm", [128, NCH, T], F32, kind="ExternalOutput").ap()
    dbg_hbm = nc.dram_tensor("dbg_hbm", [128, NCH * T], BF16, kind="ExternalOutput").ap()
    av_hbm = nc.dram_tensor("av_hbm", [4, 128, NH, 512], BF16, kind="Internal").ap()
    wo_hbm = nc.dram_tensor("wo_hbm", [NCH, 128, NH, 128], BF16, kind="Internal").ap()

    st = ExitStack()
    E_ = st.enter_context

    RT = E_(nc.sbuf_tensor("RT", [128, NCH * T], BF16))
    BIG = E_(nc.sbuf_tensor("BIG", [128, 28672], BF16))
    Rb = E_(nc.sbuf_tensor("Rb", [128, T], BF16))
    rbsb = E_(nc.sbuf_tensor("rbsb", [128, 512], BF16))
    lnrow2 = E_(nc.sbuf_tensor("lnrow2", [2, T], BF16))
    mbf = lnrow2[0:1, :]
    stg = E_(nc.sbuf_tensor("stg", [1, 512], BF16))
    scr0 = E_(nc.sbuf_tensor("scr0", [1, 512], F32))
    scr1 = E_(nc.sbuf_tensor("scr1", [1, 512], F32))
    nrec = E_(nc.sbuf_tensor("nrec", [1, 512], BF16))
    srtmp = E_(nc.sbuf_tensor("srtmp", [1, 512], F32))
    augsb = [E_(nc.sbuf_tensor(f"augsb{i}", [2, 384], BF16)) for i in range(2)]
    augwr = E_(nc.sbuf_tensor("augwr", [1, NEX], BF16))
    auga = E_(nc.sbuf_tensor("auga", [1, NEX * LR], BF16))
    boj = [E_(nc.sbuf_tensor(f"boj{i}", [1, 128], BF16)) for i in range(2)]
    ident = E_(nc.sbuf_tensor("ident", [128, 128], BF16))
    mpad = E_(nc.sbuf_tensor("mpad", [128, 1152], BF16))
    ones_f = E_(nc.sbuf_tensor("ones_f", [1, 128], F32))
    ones_b = E_(nc.sbuf_tensor("ones_b", [1, 512], BF16))
    ones_c = E_(nc.sbuf_tensor("ones_c", [128, 1], BF16))
    e_mat = E_(nc.sbuf_tensor("e_mat", [NEX, NEX * LR], BF16))
    ct = E_(nc.sbuf_tensor("ct", [128, 8], F32))

    psA = E_(nc.psum_tensor("psA", [128, 1024], F32))
    psB = E_(nc.psum_tensor("psB", [128, 1024], F32))
    psC = E_(nc.psum_tensor("psC", [128, 1024], F32))
    psE0 = E_(nc.psum_tensor("psE0", [128, 512], F32))
    psE1 = E_(nc.psum_tensor("psE1", [128, 512], F32))
    psAb = psA[:, :].bitcast(BF16)
    psCb = psC[:, :].bitcast(BF16)

    p = Plan(lambda name: E_(nc.semaphore(name)))
    for tr in p.etr.values():
        tr.h = E_(nc.semaphore(tr.name))

    B = lambda n: Buf(n)
    rt = B("rt")
    rtg = [B(f"rtg{i}") for i in range(8)]

    def rts_bufs(*cs):
        out = [rtg[c // 4] for c in cs]
        return out
    bA, bB = B("psA"), B("psB")
    bClo, bChi = B("psC_lo"), B("psC_hi")
    bE0, bE1 = B("psE0"), B("psE1")
    bRb, bmbf, bstg = B("Rb"), B("lnrow2"), B("stg")
    bscr0, bscr1, bnrec, brb = B("scr0"), B("scr1"), B("nrec"), B("rbsb")
    bsrt = B("srtmp")
    baugs = [B("augsb0"), B("augsb1")]
    baugwr, bauga = B("augwr"), B("auga")
    bboj = [B("boj0"), B("boj1")]
    cstD, cstM = B("cstD"), B("cstM")
    bavhbm, bwohbm = B("av_hbm"), B("wo_hbm")
    bwq, bwk, bwv = B("wq"), B("wk"), B("wv")
    bqT2 = [B("qT0"), B("qT1")]
    bkT2 = [B("kT0"), B("kT1")]
    bvH2 = [B("vH0"), B("vH1")]
    bvT = B("vT")
    bP = [B("P0"), B("P1")]
    bav = [B("av0"), B("av1")]
    basb, bwrsb = B("asb"), B("wrsb")
    bbsb = [B("bsb0"), B("bsb1")]
    bgmid, bgrb, blog, bgT = B("gmid"), B("grb"), B("logits"), B("gT")
    bgate = B("gatescr")
    bxsq = [B("xsq0"), B("xsq1")]
    boutst = [B("outst0"), B("outst1")]
    bavq, bwo = B("avq"), [B("wo0"), B("wo1")]
    bavqp = [B(f"avqp{i}") for i in range(4)]

    _ranges = {
        bwq: (0, 4096), bwk: (4096, 8192), bwv: (8192, 12288),
        bqT2[0]: (12288, 14336), bqT2[1]: (14336, 16384),
        bkT2[0]: (16384, 18432), bkT2[1]: (18432, 20480),
        bvH2[0]: (20480, 22528), bvH2[1]: (22528, 24576),
        bvT: (24576, 26624),
        bP[0]: (26624, 27136), bP[1]: (27136, 27648),
        bav[0]: (27648, 28160), bav[1]: (28160, 28672),
        bavqp[0]: (0, 4096), bavqp[1]: (4096, 8192),
        bavqp[2]: (8192, 12288), bavqp[3]: (12288, 16384),
        bwo[0]: (16384, 20480), bwo[1]: (20480, 24576),
        basb: (0, 2048), bwrsb: (2048, 2176),
        bbsb[0]: (2176, 3200), bbsb[1]: (3200, 4224),
        bgmid: (4224, 6272), bgrb: (6272, 8320),
        blog: (8320, 10368), bgT: (10368, 12416),
        bgate: (12416, 13056),
        bxsq[0]: (26624, 27136), bxsq[1]: (27136, 27648),
        boutst[0]: (14080, 16128), boutst[1]: (16128, 18176),
    }
    for _b, _r in _ranges.items():
        _b.rng = _r
        p.ranged.append(_b)
    bavqp2 = [B("avqp0a"), B("avqp0b")]

    wq_sb = BIG[:, 0:4096]
    wk_sb = BIG[:, 4096:8192]
    wv_sb = BIG[:, 8192:12288]
    qT2 = [BIG[:, 12288:14336], BIG[:, 14336:16384]]
    kT2 = [BIG[:, 16384:18432], BIG[:, 18432:20480]]
    vH2 = [BIG[:, 20480:22528], BIG[:, 22528:24576]]
    vT = BIG[:, 24576:26624]
    Psb = [BIG[:, 26624:27136], BIG[:, 27136:27648]]
    avsb = [BIG[:, 27648:28160], BIG[:, 28160:28672]]
    avq = BIG[:, 0:16384]
    wosb = [BIG[:, 16384:20480], BIG[:, 20480:24576]]
    a_sb = BIG[:, 0:2048]
    wr_sb = BIG[:, 2048:2176]
    b_sb = [BIG[0:64, 2176:3200], BIG[0:64, 3200:4224]]
    gmid = BIG[0:64, 4224:6272]
    grb = BIG[0:64, 6272:8320]
    logits = BIG[0:4, 8320:10368]
    gT_t = BIG[0:4, 10368:12416]
    Ltok = BIG[:, 12416:12480]
    c1t = BIG[:, 12480:12544]
    c2t = BIG[:, 12544:12608]
    c3t = BIG[:, 12608:12672]
    Gbf = BIG[:, 12672:12736]
    gf32 = BIG[:, 12736:12864].bitcast(F32)
    gf32b = BIG[:, 12864:12992].bitcast(F32)
    zt = BIG[:, 12992:13024].bitcast(F32)
    invt = BIG[:, 13024:13056].bitcast(F32)
    xsq = [BIG[:, 26624:27136], BIG[:, 27136:27648]]
    outst = [BIG[:, 14080 + 2048 * i: 14080 + 2048 * (i + 1)].bitcast(F32) for i in range(2)]

    ct_eps = ct[0:1, 0:1]
    ct_nh = ct[0:1, 1:2]
    ct_ph = ct[0:1, 2:3]
    ct_es = ct[:, 3:4]
    ct_eb = ct[:, 4:5]
    ct_gb = ct[:, 5:6]
    ct_m1 = ct[0:1, 6:7]

    RTs = lambda c, o, w: RT[:, T * c + o: T * c + o + w]


    def vmemsets(v):
        v.memset(ones_c[:, :], 1.0)
        v.memset(ct[:, 0:1], EPS)
        v.memset(ct[:, 1:2], -0.5)
        v.memset(ct[:, 2:3], 0.5)
        v.memset(ct[:, 3:4], EXP_SCALE)
        v.memset(ct[:, 4:5], EXP_BIAS)
        v.memset(ct[:, 5:6], GATE_BIAS)
        v.memset(Psb[0], 0.0)
        v.memset(Psb[1], 0.0)
        return v.memset(ct[:, 6:7], -1.0)

    p.step("vector", vmemsets, writes=[cstM, bP[0], bP[1]])

    for i in range(4):
        p.step("gpsimd",
               lambda g, i=i: g.dma_start(
                   RT[:, 16384 * i: 16384 * (i + 1)],
                   xh[:, 8 * i: 8 * (i + 1), :].rearrange("p c t -> p (c t)")),
               writes=[rt, rtg[2 * i], rtg[2 * i + 1]], dma=True, dsem=rtg[2 * i])
    p.step("gpsimd", lambda g: g.dma_start(ident[:, :], eye_a), writes=[cstD], dma=True, dsem=cstD)
    p.step("gpsimd", lambda g: g.dma_start(mpad[:, :], mpad_a), writes=[cstD], dma=True, dsem=cstD)
    p.step("gpsimd", lambda g: g.dma_start(ones_f[:, :], ones_a[:, 0:128]), writes=[cstD], dma=True, dsem=cstD)
    p.step("gpsimd", lambda g: g.dma_start(ones_b[:, :], ones_a), writes=[cstD], dma=True, dsem=cstD)
    p.step("gpsimd", lambda g: g.dma_start(e_mat[:, :], e_a), writes=[cstD], dma=True, dsem=cstD)
    p.step("gpsimd", lambda g: g.dma_start(augsb[0][1:2, :], bias_a[0:1, :]), writes=[baugs[0]], dma=True, dsem=baugs[0])
    p.step("gpsimd", lambda g: g.dma_start(augsb[1][1:2, :], bias_a[1:2, :]), writes=[baugs[1]], dma=True, dsem=baugs[1])

    def plan_stats_tq(tq):
        import os as _os
        nch = int(_os.environ.get("KMK_NCH", "32"))
        fin = int(_os.environ.get("KMK_FIN", "9"))
        if True:
            o = 512 * tq
            for c in range(nch):
                par = c % 2
                p.step("vector",
                       lambda v, c=c, o=o, par=par: v.tensor_tensor(
                           xsq[par], RTs(c, o, 512), RTs(c, o, 512), ALU.mult),
                       reads=rts_bufs(c), writes=[bxsq[par], bP[par]])

                def mms(t, c=c, o=o, par=par):
                    t.matmul(psC[0:1, 0:512], ones_c[:, :], RTs(c, o, 512),
                             start=(c == 0), stop=(c == nch - 1))
                    return t.matmul(psC[0:1, 512:1024], ones_c[:, :], xsq[par],
                                    start=(c == 0), stop=(c == nch - 1))
                p.step("tensor", mms, reads=rts_bufs(c) + [bxsq[par], cstM],
                       writes=[bClo, bChi])
            if not fin:
                return
            if fin >= 1:
                p.step("vector",
                       lambda v: v.tensor_scalar_mul(scr0[0:1, 0:512], psC[0:1, 0:512], 1.0 / D),
                       reads=[bClo], writes=[bscr0])
            if fin >= 2:
                p.step("vector",
                       lambda v, o=o: v.tensor_copy(mbf[0:1, o:o + 512], scr0[0:1, 0:512]),
                       reads=[bscr0], writes=[bmbf])
            if fin >= 3:
                p.step("vector",
                       lambda v: v.tensor_tensor(scr1[0:1, 0:512], scr0[0:1, 0:512],
                                                 scr0[0:1, 0:512], ALU.mult),
                       reads=[bscr0], writes=[bscr1])
            if fin >= 4:
                p.step("vector",
                       lambda v: v.scalar_tensor_tensor(scr0[0:1, 0:512], psC[0:1, 512:1024],
                                                        1.0 / D, scr1[0:1, 0:512],
                                                        ALU.mult, ALU.subtract),
                       reads=[bChi, bscr1], writes=[bscr0])
            if fin >= 5:
                p.step("scalar",
                       lambda s: s.activation(scr1[0:1, 0:512], scr0[0:1, 0:512], AF.Ln,
                                              bias=ct_eps),
                       reads=[bscr0, cstM], writes=[bscr1])
            if fin >= 6:
                p.step("scalar",
                       lambda s: s.activation(scr0[0:1, 0:512], scr1[0:1, 0:512], AF.Exp,
                                              scale=ct_nh),
                       reads=[bscr1, cstM], writes=[bscr0])
            if fin >= 7:
                p.step("scalar",
                       lambda s: s.activation(stg[0:1, 0:512], scr1[0:1, 0:512],
                                              AF.Exp, scale=ct_ph),
                       reads=[bscr1, cstM], writes=[bstg])
                p.step("gpsimd",
                       lambda g, o=o: g.dma_start(lnrow2[1:2, o:o + 512], stg[0:1, 0:512]),
                       reads=[bstg, bmbf], writes=[bmbf], dma=True, dsem=bmbf)
            if fin >= 8:
                p.step("tensor",
                       lambda t: t.matmul(psE0[:, :], ones_f[:, :], scr0[0:1, 0:512],
                                          start=True, stop=True),
                       reads=[bscr0, cstD], writes=[bE0])
            if fin >= 9:
                p.step("vector",
                       lambda v, o=o: v.tensor_copy(Rb[:, o:o + 512], psE0[:, :]),
                       reads=[bE0], writes=[bRb])

    def plan_stats_couter():
        sumv = [psA[0:1, 0:512], psA[0:1, 512:1024], psB[0:1, 0:512], psB[0:1, 512:1024]]
        ssqv = [psC[0:1, 0:512], psC[0:1, 512:1024], psE0[0:1, :], psE1[0:1, :]]
        sumb = [bA, bA, bB, bB]
        ssqb = [bClo, bChi, bE0, bE1]
        for c in range(NCH):
            for tq in range(4):
                o = 512 * tq
                par = (4 * c + tq) % 2
                p.step("vector",
                       lambda v, c=c, o=o, par=par: v.tensor_tensor(
                           xsq[par], RTs(c, o, 512), RTs(c, o, 512), ALU.mult),
                       reads=rts_bufs(c), writes=[bxsq[par], bP[par]])

                def mms(t, c=c, o=o, par=par, tq=tq):
                    t.matmul(sumv[tq], ones_c[:, :], RTs(c, o, 512),
                             start=(c == 0), stop=(c == NCH - 1))
                    return t.matmul(ssqv[tq], ones_c[:, :], xsq[par],
                                    start=(c == 0), stop=(c == NCH - 1))
                p.step("tensor", mms, reads=rts_bufs(c) + [bxsq[par], cstM],
                       writes=[sumb[tq], ssqb[tq]])
        for tq in range(4):
            o = 512 * tq
            p.step("vector",
                   lambda v, tq=tq: v.tensor_scalar_mul(scr0[0:1, 0:512], sumv[tq], 1.0 / D),
                   reads=[sumb[tq]], writes=[bscr0])
            p.step("vector",
                   lambda v, o=o: v.tensor_copy(mbf[0:1, o:o + 512], scr0[0:1, 0:512]),
                   reads=[bscr0], writes=[bmbf])
            p.step("vector",
                   lambda v: v.tensor_tensor(scr1[0:1, 0:512], scr0[0:1, 0:512],
                                             scr0[0:1, 0:512], ALU.mult),
                   reads=[bscr0], writes=[bscr1])
            p.step("vector",
                   lambda v, tq=tq: v.scalar_tensor_tensor(scr0[0:1, 0:512], ssqv[tq],
                                                           1.0 / D, scr1[0:1, 0:512],
                                                           ALU.mult, ALU.subtract),
                   reads=[ssqb[tq], bscr1], writes=[bscr0])
            p.step("scalar",
                   lambda s: s.activation(scr1[0:1, 0:512], scr0[0:1, 0:512], AF.Ln,
                                          bias=ct_eps),
                   reads=[bscr0, cstM], writes=[bscr1])
            p.step("scalar",
                   lambda s: s.activation(scr0[0:1, 0:512], scr1[0:1, 0:512], AF.Exp,
                                          scale=ct_nh),
                   reads=[bscr1, cstM], writes=[bscr0])
            p.step("scalar",
                   lambda s: s.activation(stg[0:1, 0:512], scr1[0:1, 0:512],
                                          AF.Exp, scale=ct_ph),
                   reads=[bscr1, cstM], writes=[bstg])
            p.step("gpsimd",
                   lambda g, o=o: g.dma_start(lnrow2[1:2, o:o + 512], stg[0:1, 0:512]),
                   reads=[bstg, bmbf], writes=[bmbf], dma=True, dsem=bmbf)
            rbv = [psC[:, 0:512], psC[:, 512:1024], psE0[:, :], psE1[:, :]][tq]
            rbb = [bClo, bChi, bE0, bE1][tq]
            p.step("tensor",
                   lambda t, rbv=rbv: t.matmul(rbv, ones_f[:, :], scr0[0:1, 0:512],
                                               start=True, stop=True),
                   reads=[bscr0, cstD], writes=[rbb])
            p.step("vector",
                   lambda v, o=o, rbv=rbv: v.tensor_copy(Rb[:, o:o + 512], rbv),
                   reads=[rbb], writes=[bRb])

    def plan_ffn(final, sub="all", skip_stats=False):
        order = ["dma", "stats", "spre", "router", "mid", "gates", "delta"]
        lim = order.index(sub) if sub in order else 99

        def upto(k):
            return order.index(k) <= lim
        p.step("gpsimd", lambda g: g.dma_start(a_sb, a_a.rearrange("p c j -> p (c j)")),
               writes=[basb], dma=True, dsem=basb)
        p.step("gpsimd", lambda g: g.dma_start(wr_sb, wr_a.rearrange("p c j -> p (c j)")),
               writes=[bwrsb], dma=True, dsem=bwrsb)
        p.step("gpsimd", lambda g: g.dma_start(b_sb[0], b_a[:, 0:1024]),
               writes=[bbsb[0]], dma=True, dsem=bbsb[0])

        if not upto("stats"):
            return
        if not skip_stats:
            if not final:
                plan_stats_couter()
            else:
                for _tq in range(4):
                    plan_stats_tq(_tq)
        if not upto("spre"):
            return

        def swr(t):
            mm = None
            for c in range(NCH):
                mm = t.matmul(psA[0:1, 0:NEX], ones_c[:, :], wr_sb[:, NEX * c:NEX * (c + 1)],
                              start=(c == 0), stop=(c == NCH - 1))
            return mm
        p.step("tensor", swr, reads=[bwrsb, cstM], writes=[bA])

        def sa(t):
            mm = None
            for c in range(NCH):
                mm = t.matmul(psB[0:1, 0:64], ones_c[:, :], a_sb[:, 64 * c:64 * (c + 1)],
                              start=(c == 0), stop=(c == NCH - 1))
            return mm
        p.step("tensor", sa, reads=[basb, cstM], writes=[bB])
        p.step("vector", lambda v: v.tensor_scalar_mul(augwr[:, :], psA[0:1, 0:NEX], -1.0),
               reads=[bA], writes=[baugwr])
        p.step("vector", lambda v: v.tensor_scalar_mul(auga[:, :], psB[0:1, 0:64], -1.0),
               reads=[bB], writes=[bauga])

        if not upto("router"):
            return
        for tq in range(4):
            o = 512 * tq

            def rmm(t, o=o):
                for c in range(NCH):
                    t.matmul(psE0[0:4, :], wr_sb[:, NEX * c:NEX * (c + 1)],
                             RTs(c, o, 512), start=(c == 0), stop=False)
                return t.matmul(psE0[0:4, :], augwr[:, :], mbf[0:1, o:o + 512],
                                start=False, stop=True)
            p.step("tensor", rmm, reads=[rt, bwrsb, baugwr, bmbf] + rtg, writes=[bE0])
            p.step("vector",
                   lambda v, o=o: v.tensor_tensor(logits[:, o:o + 512], psE0[0:4, :],
                                                  Rb[0:4, o:o + 512], ALU.mult),
                   reads=[bE0, bRb], writes=[blog])

        if not upto("mid"):
            return
        if not upto("gates"):
            return
        def ltr(t):
            mm = None
            for i in range(16):
                mm = t.transpose(psCb[:, 4 * i:4 * (i + 1)], logits[:, 128 * i:128 * (i + 1)],
                                 ident[0:4, 0:4])
            return mm
        p.step("tensor", ltr, reads=[blog, cstD], writes=[bClo, bChi])
        p.step("vector", lambda v: v.tensor_copy(Ltok, psCb[:, 0:64]),
               reads=[bClo, bChi], writes=[bgate])

        for half, (ps, bps) in enumerate([(psA, bA), (psB, bB)]):
            ho = 1024 * half

            def midmm(t, ho=ho, ps=ps):
                for c in range(NCH):
                    t.matmul(ps[0:64, 0:512], a_sb[:, 64 * c:64 * (c + 1)],
                             RTs(c, ho, 512), start=(c == 0), stop=False)
                    t.matmul(ps[0:64, 512:1024], a_sb[:, 64 * c:64 * (c + 1)],
                             RTs(c, ho + 512, 512), start=(c == 0), stop=False)
                t.matmul(ps[0:64, 0:512], auga[:, :], mbf[0:1, ho:ho + 512],
                         start=False, stop=True)
                return t.matmul(ps[0:64, 512:1024], auga[:, :], mbf[0:1, ho + 512:ho + 1024],
                                start=False, stop=True)
            p.step("tensor", midmm, reads=[rt, basb, bauga, bmbf] + rtg, writes=[bps])

        lv = lambda ap, e: ap.rearrange("p (i e) -> p i e", e=4)[:, :, e]
        gv = lambda e: gf32b.rearrange("p (i e) -> p i e", e=4)[:, :, e]

        def cmps(v):
            last = None
            for e in range(4):
                tgts = [c1t, c2t, c3t]
                k = 0
                for j in range(4):
                    if j == e:
                        continue
                    op = ALU.is_ge if j < e else ALU.is_gt
                    last = v.tensor_tensor(lv(tgts[k], e), lv(Ltok, j), lv(Ltok, e), op)
                    k += 1
            return last
        p.step("vector", cmps, reads=[bgate], writes=[bgate])
        p.step("vector", lambda v: v.tensor_tensor(c1t, c1t, c2t, ALU.add),
               reads=[bgate], writes=[bgate])
        p.step("vector", lambda v: v.tensor_tensor(c1t, c1t, c3t, ALU.add),
               reads=[bgate], writes=[bgate])
        p.step("vector", lambda v: v.tensor_scalar(c2t, c1t, 1.5, None, ALU.is_le),
               reads=[bgate], writes=[bgate])
        p.step("scalar",
               lambda s: s.activation(gf32[:, :], Ltok, AF.Exp, bias=ct_gb),
               reads=[bgate, cstM], writes=[bgate])
        p.step("vector", lambda v: v.tensor_tensor(gf32b[:, :], gf32[:, :], c2t, ALU.mult),
               reads=[bgate], writes=[bgate])
        p.step("vector", lambda v: v.tensor_tensor(zt[:, :], gv(0), gv(1), ALU.add),
               reads=[bgate], writes=[bgate])
        p.step("vector", lambda v: v.tensor_tensor(zt[:, :], zt[:, :], gv(2), ALU.add),
               reads=[bgate], writes=[bgate])
        p.step("vector", lambda v: v.tensor_tensor(zt[:, :], zt[:, :], gv(3), ALU.add),
               reads=[bgate], writes=[bgate])
        p.step("vector", lambda v: v.reciprocal(invt[:, :], zt[:, :]),
               reads=[bgate], writes=[bgate])

        def gmuls(v):
            last = None
            for e in range(4):
                last = v.tensor_tensor(lv(Gbf, e), gv(e), invt[:, :], ALU.mult)
            return last
        p.step("vector", gmuls, reads=[bgate], writes=[bgate])

        def gtr(t):
            mm = None
            for i in range(16):
                mm = t.transpose(psCb[0:4, 128 * i:128 * (i + 1)], Gbf[:, 4 * i:4 * (i + 1)],
                                 ident[:, :])
            return mm
        p.step("tensor", gtr, reads=[bgate, cstD], writes=[bClo, bChi])
        p.step("vector", lambda v: v.tensor_copy(gT_t[:, 0:1024], psCb[0:4, 0:1024]),
               reads=[bClo, bChi], writes=[bgT])
        p.step("vector", lambda v: v.tensor_copy(gT_t[:, 1024:2048], psCb[0:4, 1024:2048]),
               reads=[bClo, bChi], writes=[bgT])

        for half, bmid in enumerate([bA, bB]):
            ho = 1024 * half
            psM = psA if half == 0 else psB

            def emm(t, ho=ho):
                t.matmul(psC[0:64, 0:512], e_mat[:, :], gT_t[:, ho:ho + 512],
                         start=True, stop=True)
                return t.matmul(psC[0:64, 512:1024], e_mat[:, :], gT_t[:, ho + 512:ho + 1024],
                                start=True, stop=True)
            p.step("tensor", emm, reads=[bgT, cstD], writes=[bClo, bChi])
            p.step("vector",
                   lambda v, ho=ho: v.tensor_tensor(grb[:, ho:ho + 1024], psC[0:64, 0:1024],
                                                    Rb[0:64, ho:ho + 1024], ALU.mult),
                   reads=[bClo, bChi, bRb], writes=[bgrb])
            p.step("vector",
                   lambda v, ho=ho, psM=psM: v.tensor_tensor(
                       gmid[:, ho:ho + 1024], psM[0:64, 0:1024], grb[:, ho:ho + 1024],
                       ALU.mult),
                   reads=[bmid, bgrb], writes=[bgmid])

        if not upto("delta"):
            return
        for g in range(4):
            if g < 3:
                p.step("gpsimd",
                       lambda gg, g=g: gg.dma_start(b_sb[(g + 1) % 2],
                                                    b_a[:, 1024 * (g + 1):1024 * (g + 2)]),
                       writes=[bbsb[(g + 1) % 2]], dma=True, dsem=bbsb[(g + 1) % 2])
            for j in range(8 * g, 8 * g + 8):
                jo = 128 * (j % 8)

                def dmm(t, j=j, jo=jo, g=g):
                    bs = b_sb[g % 2]
                    t.matmul(psA[:, 0:512], bs[:, jo:jo + 128], gmid[:, 0:512],
                             start=True, stop=True)
                    t.matmul(psA[:, 512:1024], bs[:, jo:jo + 128], gmid[:, 512:1024],
                             start=True, stop=True)
                    t.matmul(psB[:, 0:512], bs[:, jo:jo + 128], gmid[:, 1024:1536],
                             start=True, stop=True)
                    return t.matmul(psB[:, 512:1024], bs[:, jo:jo + 128], gmid[:, 1536:2048],
                                    start=True, stop=True)
                p.step("tensor", dmm, reads=[bbsb[g % 2], bgmid], writes=[bA, bB])
                if not final:
                    p.step("vector",
                           lambda v, j=j: v.tensor_tensor(RTs(j, 0, 1024), RTs(j, 0, 1024),
                                                          psA[:, :], ALU.add),
                           reads=[bA], writes=[rt] + rts_bufs(j))
                    p.step("vector",
                           lambda v, j=j: v.tensor_tensor(RTs(j, 1024, 1024),
                                                          RTs(j, 1024, 1024),
                                                          psB[:, :], ALU.add),
                           reads=[bB], writes=[rt] + rts_bufs(j))
                else:
                    for half, bp in enumerate([bA, bB]):
                        par = (j + half) % 2
                        ps = psA if half == 0 else psB
                        p.step("vector",
                               lambda v, j=j, half=half, par=par, ps=ps: v.tensor_tensor(
                                   outst[par][:, 0:1024], RTs(j, 1024 * half, 1024),
                                   ps[:, :], ALU.add),
                               reads=[bp, rt] + rts_bufs(j), writes=[boutst[par]])
                        p.step("sync",
                               lambda y, j=j, half=half, par=par: y.dma_start(
                                   out_hbm[:, j, 1024 * half:1024 * (half + 1)],
                                   outst[par][:, 0:1024]),
                               reads=[boutst[par]], dma=True, dsem=boutst[par])

    def plan_head(h):
        par_h = h % 2
        qT = qT2[par_h]
        kT = kT2[par_h]
        vH = vH2[par_h]
        bqT, bkT, bvH = bqT2[par_h], bkT2[par_h], bvH2[par_h]
        proj_t = []
        projs = [
            ("q", wq_sb, bwq, qT, bqT, 0),
            ("k", wk_sb, bwk, kT, bkT, 128),
            ("v", wv_sb, bwv, vT, bvT, 256),
        ]
        for pi, (nm, wsb, bw, oT, boT, so) in enumerate(projs):
            ppar = pi % 2
            def srow(t, wsb=wsb):
                mm = None
                for g in range(8):
                    mm = t.matmul(psC[0:1, 512:1024], ones_c[:, :],
                                  wsb[:, 512 * g:512 * (g + 1)],
                                  start=(g == 0), stop=(g == 7))
                return mm
            proj_t.append(lambda srow=srow, bw=bw: p.step(
                "tensor", srow, reads=[bw, cstM], writes=[bChi]))
            proj_t.append(lambda: p.step(
                "vector",
                lambda v: v.tensor_copy(srtmp[0:1, 0:512], psC[0:1, 512:1024]),
                reads=[bChi], writes=[bsrt]))
            proj_t.append(lambda: p.step(
                "vector",
                lambda v: v.tensor_tensor(srtmp[0:1, 0:256], srtmp[0:1, 0:256],
                                          srtmp[0:1, 256:512], ALU.add),
                reads=[bsrt], writes=[bsrt]))
            proj_t.append(lambda so=so: p.step(
                "vector",
                lambda v: v.scalar_tensor_tensor(augsb[par_h][0:1, so:so + 128],
                                                 srtmp[0:1, 0:128], -1.0,
                                                 srtmp[0:1, 128:256],
                                                 ALU.mult, ALU.subtract),
                reads=[bsrt], writes=[baugs[par_h]]))
            for c0 in range(0, NCH, 2):
                def cmm(t, c0=c0, wsb=wsb):
                    mm = None
                    for c in (c0, c0 + 1):
                        wch = wsb[:, 128 * c:128 * (c + 1)]
                        t.matmul(psA[:, 0:512], wch, RTs(c, 0, 512), start=(c == 0), stop=False)
                        t.matmul(psA[:, 512:1024], wch, RTs(c, 512, 512), start=(c == 0), stop=False)
                        t.matmul(psB[:, 0:512], wch, RTs(c, 1024, 512), start=(c == 0), stop=False)
                        mm = t.matmul(psB[:, 512:1024], wch, RTs(c, 1536, 512),
                                      start=(c == 0), stop=False)
                    return mm
                proj_t.append(lambda c0=c0, cmm=cmm, bw=bw: p.step(
                    "tensor", cmm, reads=rts_bufs(c0, c0 + 1) + [rt, bw], writes=[bA, bB]))
            if h + 1 < NH:
                proj_t.append(lambda pi=pi, wsb=wsb, bw=bw: p.step(
                    "gpsimd",
                    lambda g: g.dma_start(wsb, wqkv[h + 1, pi].rearrange("p c j -> p (c j)")),
                    writes=[bw], dma=True, dsem=bw))

            def augmm(t, so=so):
                ab = augsb[par_h]
                mm = None
                for half, ps in enumerate([psA, psB]):
                    ho = 1024 * half
                    t.matmul(ps[:, 0:512], ab[0:2, so:so + 128], lnrow2[0:2, ho:ho + 512],
                             start=False, stop=True)
                    mm = t.matmul(ps[:, 512:1024], ab[0:2, so:so + 128],
                                  lnrow2[0:2, ho + 512:ho + 1024], start=False, stop=True)
                return mm
            proj_t.append(lambda augmm=augmm: p.step(
                "tensor", augmm, reads=[baugs[par_h], bmbf], writes=[bA, bB]))
            for half, (ps, bp) in enumerate([(psA, bA), (psB, bB)]):
                ho = 1024 * half
                proj_t.append(lambda ps=ps, bp=bp, ho=ho, oT=oT, boT=boT: p.step(
                    "vector",
                    lambda v: v.tensor_tensor(oT[:, ho:ho + 1024], ps[:, :],
                                              Rb[:, ho:ho + 1024], ALU.mult),
                    reads=[bp, bRb], writes=[boT]))
        if h + 1 < NH:
            proj_t.append(lambda: p.step(
                "gpsimd",
                lambda g: g.dma_start(augsb[(h + 1) % 2][1:2, :], bias_a[h + 1:h + 2, :]),
                writes=[baugs[(h + 1) % 2]], dma=True, dsem=baugs[(h + 1) % 2]))
        proj_t.append(lambda: p.step(
            "gpsimd",
            lambda g: g.dma_start(wo_hbm[h].rearrange("b c d -> b (c d)"),
                                  wo_a[h].rearrange("b c d -> b (c d)")),
            writes=[bwohbm], dma=True, dsem=bwohbm))

        def vtr(t):
            mm = None
            for i in range(16):
                mm = t.transpose(psAb[:, 128 * i:128 * (i + 1)],
                                 vT[:, 128 * i:128 * (i + 1)], ident[:, :])
            return mm
        proj_t.append(lambda: p.step("tensor", vtr, reads=[bvT, cstD], writes=[bA]))
        proj_t.append(lambda vH=vH, bvH=bvH: p.step(
            "vector", lambda v: v.tensor_copy(vH[:, :], psAb[:, :]),
            reads=[bA], writes=[bvH]))

        attn_t = []

        def smm_step(m4, c, qT=qT, kT=kT, bqT=bqT, bkT=bkT):
            j0 = max(0, c - 4 * m4)
            w = 512 - 128 * j0

            def smm(t, m4=m4, c=c, w=w):
                return t.matmul(psC[:, 0:w], kT[:, 128 * c:128 * (c + 1)],
                                qT[:, 512 * m4 + 512 - w: 512 * (m4 + 1)],
                                start=True, stop=True)
            p.step("tensor", smm, reads=[bkT, bqT], writes=[bClo])

        for m4 in range(4):
            nch = 4 * (m4 + 1)
            if m4 == 0:
                attn_t.append(lambda: smm_step(0, 0))
            for c in range(nch):
                par = c % 2
                j0 = max(0, c - 4 * m4)
                rs = 128 * j0
                diag = c >= 4 * m4

                w = 512 - rs

                def expst(par=par, w=w, rs=rs):
                    p.step("scalar",
                           lambda s: s.activation(Psb[par][:, rs:512], psC[:, 0:w],
                                                  AF.Exp, bias=ct_eb, scale=ct_es),
                           reads=[bClo, cstM], writes=[bP[par]])
                attn_t.append(expst)
                if diag:
                    mw = rs + 128

                    def maskst(par=par, mw=mw):
                        p.step("vector",
                               lambda v: v.tensor_tensor(Psb[par][:, 0:mw],
                                                         Psb[par][:, 0:mw],
                                                         mpad[:, 1152 - mw:1152], ALU.mult),
                               reads=[cstD], writes=[bP[par]])
                    attn_t.append(maskst)
                if c + 1 < nch:
                    attn_t.append(lambda m4=m4, c=c: smm_step(m4, c + 1))

                def savst(c=c, par=par, m4=m4, nch=nch, vH=vH, bvH=bvH):
                    def sav(t):
                        return t.matmul(psE0[:, :], vH[:, 128 * c:128 * (c + 1)],
                                        Psb[par][:, :], start=(c == 0), stop=(c == nch - 1))
                    p.step("tensor", sav, reads=[bP[par], bvH], writes=[bE0])
                attn_t.append(savst)
                if c % 2 == 1 or c == nch - 1:
                    cpair = [c - 1, c] if c % 2 == 1 else [c]

                    def sumst(cpair=cpair, nch=nch):
                        def summ(t):
                            mm = None
                            for cc in cpair:
                                mm = t.matmul(psE1[0:1, :], ones_c[:, :], Psb[cc % 2][:, :],
                                              start=(cc == 0), stop=(cc == nch - 1))
                            return mm
                        p.step("tensor", summ,
                               reads=[bP[cc % 2] for cc in cpair] + [cstM], writes=[bE1])
                    attn_t.append(sumst)

            if m4 < 3:
                attn_t.append(lambda m4=m4: smm_step(m4 + 1, 0))

            def tailst(m4=m4):
                p.step("scalar",
                       lambda s: s.activation(scr0[0:1, 0:512], psE1[0:1, :], AF.Ln),
                       reads=[bE1], writes=[bscr0])
                p.step("scalar",
                       lambda s: s.activation(nrec[0:1, 0:512], scr0[0:1, 0:512], AF.Exp,
                                              scale=ct_m1),
                       reads=[bscr0, cstM], writes=[bnrec])
                p.step("tensor",
                       lambda t: t.matmul(psE1[:, :], ones_b[0:1, 0:128], nrec[0:1, 0:512],
                                          start=True, stop=True),
                       reads=[bnrec, cstD], writes=[bE1])
                p.step("vector", lambda v: v.tensor_copy(rbsb[:, :], psE1[:, :]),
                       reads=[bE1], writes=[brb])
                par = (h * 4 + m4) % 2
                p.step("vector",
                       lambda v, par=par: v.tensor_tensor(avsb[par][:, :], psE0[:, :],
                                                          rbsb[:, :], ALU.mult),
                       reads=[bE0, brb], writes=[bav[par]])
                p.step("sync",
                       lambda y, par=par: y.dma_start(av_hbm[m4, :, h, :], avsb[par][:, :]),
                       reads=[bav[par]], writes=[bavhbm], dma=True, dsem=bav[par])
            attn_t.append(tailst)
        return proj_t, attn_t

    def emit_interleaved(a, b):
        if not b:
            for f in a:
                f()
            return
        ratio = len(a) / len(b)
        bi = 0.0
        j = 0
        for i, f in enumerate(a):
            f()
            while (i + 1) / ratio >= j + 1 and j < len(b):
                b[j]()
                j += 1
        while j < len(b):
            b[j]()
            j += 1

    def plan_oproj():
        def load_avq_piece(tq, pc):
            p.step("gpsimd",
                   lambda g, tq=tq, pc=pc: g.dma_start(
                       avq[:, 4096 * pc:4096 * (pc + 1)],
                       av_hbm[tq, :, 8 * pc:8 * (pc + 1), :].rearrange("p h j -> p (h j)")),
                   reads=[bavhbm], writes=[bavqp[pc]], dma=True, dsem=bavqp[pc])

        for tq in range(4):
            for pc in range(4):
                load_avq_piece(tq, pc)
            if tq == 0:
                p.step("gpsimd",
                       lambda g: g.dma_start(
                           wosb[0], wo_hbm[0].rearrange("p h j -> p (h j)")),
                       reads=[bwohbm], writes=[bwo[0]], dma=True, dsem=bwo[0])
                p.step("gpsimd", lambda g: g.dma_start(boj[0][:, :], bo_a[0:1, :]),
                       writes=[bboj[0]], dma=True, dsem=bboj[0])
            js = list(range(NCH)) if tq % 2 == 0 else list(range(NCH - 1, -1, -1))
            for ji, j in enumerate(js):
                par = j % 2
                ps = psA if par == 0 else psB
                bp = bA if par == 0 else bB
                if ji + 1 < NCH:
                    jn = js[ji + 1]
                    p.step("gpsimd",
                           lambda g, jn=jn: g.dma_start(
                               wosb[jn % 2], wo_hbm[jn].rearrange("p h j -> p (h j)")),
                           reads=[bwohbm], writes=[bwo[jn % 2]], dma=True, dsem=bwo[jn % 2])
                    p.step("gpsimd",
                           lambda g, jn=jn: g.dma_start(boj[jn % 2][:, :],
                                                        bo_a[jn:jn + 1, :]),
                           writes=[bboj[jn % 2]], dma=True, dsem=bboj[jn % 2])

                for pc in range(4):
                    def ommp(t, j=j, par=par, ps=ps, pc=pc):
                        w = wosb[par]
                        mm = None
                        for hh in range(8 * pc, 8 * (pc + 1)):
                            mm = t.matmul(ps[:, 0:512], w[:, 128 * hh:128 * (hh + 1)],
                                          avq[:, 512 * hh:512 * (hh + 1)],
                                          start=(hh == 0), stop=False)
                        return mm
                    p.step("tensor", ommp, reads=[bavqp[pc], bwo[par], cstD], writes=[bp])
                p.step("tensor",
                       lambda t, par=par, ps=ps: t.matmul(ps[:, 0:512], boj[par][:, :],
                                                          ones_b[:, :], start=False, stop=True),
                       reads=[bboj[par], cstD], writes=[bp])
                p.step("vector",
                       lambda v, j=j, tq=tq, ps=ps: v.tensor_tensor(
                           RTs(j, 512 * tq, 512), RTs(j, 512 * tq, 512), ps[:, 0:512],
                           ALU.add),
                       reads=[bp], writes=[rt] + rts_bufs(j))
            plan_stats_tq(tq)

    def dump_rt():
        p.step("sync", lambda y: y.dma_start(dbg_hbm, RT[:, :]), reads=[rt],
               dma=True, dsem=rt)

    import os
    stage = int(os.environ.get("KMK_STAGE", "6"))
    sub = os.environ.get("KMK_SUB", "all")
    dbg = int(os.environ.get("KMK_DBG", "0"))
    plan_ffn(final=False, sub=sub)
    if dbg == 1:
        dump_rt()
    if stage >= 2:
        for _tq in range(4):
            plan_stats_tq(_tq)
    if stage >= 3:
        for pi, (wsb, wb) in enumerate([(wq_sb, bwq), (wk_sb, bwk), (wv_sb, bwv)]):
            p.step("gpsimd",
                   lambda g, pi=pi, wsb=wsb: g.dma_start(
                       wsb, wqkv[0, pi].rearrange("p c j -> p (c j)")),
                   writes=[wb], dma=True, dsem=wb)
        nh = 1 if stage == 3 else NH
        prev_attn = []
        for h in range(nh):
            proj_t, attn_t = plan_head(h)
            emit_interleaved(proj_t, prev_attn)
            prev_attn = attn_t
        for f in prev_attn:
            f()
    if stage >= 5:
        plan_oproj()
        if dbg == 2:
            dump_rt()
    if stage >= 6:
        plan_ffn(final=True, skip_stats=True)

    for tr in p.dsems:
        p.ops["sync"].append(lambda e, h=tr, n=tr.n: e.wait_ge(h.h, n))

    block = E_(nc.Block())
    p.emit(block)
    st.close()
    return nc


def _prep(inputs):
    f = lambda k: np.asarray(inputs[k], np.float32)
    x = f("L0_attn_x")
    Wq, Wk, Wv, Wo = f("L0_attn_Wq"), f("L0_attn_Wk"), f("L0_attn_Wv"), f("L0_attn_Wo")
    bq, bk, bv, bo = f("L0_attn_bq"), f("L0_attn_bk"), f("L0_attn_bv"), f("L0_attn_bo")
    A, Bm, Wr = f("L0_ffn_A"), f("L0_ffn_B"), f("L0_ffn_Wr")

    def wblocks(W):
        WT = np.ascontiguousarray(W.T)
        return np.ascontiguousarray(
            WT.reshape(NCH, 128, NH, 128).transpose(2, 1, 0, 3))

    wq_b, wk_b, wv_b = wblocks(Wq), wblocks(Wk), wblocks(Wv)
    wqkv = np.ascontiguousarray(np.stack([wq_b, wk_b, wv_b], axis=1))
    WoT = np.ascontiguousarray(Wo.T)
    wo_a = np.ascontiguousarray(
        WoT.reshape(NH, 128, NCH, 128).transpose(2, 1, 0, 3))
    a_a = np.ascontiguousarray(
        A.transpose(2, 0, 1).reshape(D, NEX * LR).reshape(NCH, 128, NEX * LR)
        .transpose(1, 0, 2))
    b_a = np.ascontiguousarray(Bm.transpose(0, 2, 1).reshape(NEX * LR, D))
    wr_a = np.ascontiguousarray(
        Wr.T.reshape(NCH, 128, NEX).transpose(1, 0, 2))
    bias_a = np.ascontiguousarray(
        np.stack([bq.reshape(NH, 128), bk.reshape(NH, 128), bv.reshape(NH, 128)],
                 axis=1).reshape(NH, 3 * 128))
    bo_a = np.ascontiguousarray(bo.reshape(NCH, 128))
    eye_a = np.eye(128, dtype=np.float32)
    kk = np.arange(128)[:, None]
    col = np.arange(1152)[None, :]
    mpad_a = ((col - 1024) >= kk).astype(np.float32)
    ones_a = np.ones((1, 512), np.float32)
    e_a = np.zeros((NEX, NEX * LR), np.float32)
    for e in range(NEX):
        e_a[e, LR * e:LR * (e + 1)] = 2.0

    shared = dict(wqkv=wqkv, wo_a=wo_a, a_a=a_a, b_a=b_a, wr_a=wr_a,
                  bias_a=bias_a, bo_a=bo_a, eye_a=eye_a, mpad_a=mpad_a,
                  ones_a=ones_a, e_a=e_a)
    in_maps = []
    for b in range(NCORES):
        xh = np.ascontiguousarray(
            x[b].T.reshape(NCH, 128, T).transpose(1, 0, 2))
        m = dict(shared)
        m["xh"] = xh
        in_maps.append(m)
    return in_maps


def kernel(**inputs):
    in_maps = _prep(inputs)
    nc = build_nc()
    res = run_bass_kernel_spmd(nc, in_maps, core_ids=list(range(NCORES)))
    outs = []
    for r in res.results:
        o = r["out_hbm"]
        outs.append(np.ascontiguousarray(o.transpose(1, 0, 2).reshape(D, T).T))
    return np.stack(outs).astype(np.float32)


if __name__ == "__main__":
    rng = np.random.default_rng(0)
    ins = {
        "L0_attn_x": rng.standard_normal((8, T, D), dtype=np.float32),
        "L0_attn_Wq": (rng.standard_normal((D, D), dtype=np.float32) / 64),
        "L0_attn_bq": (rng.standard_normal((D,), dtype=np.float32) / 64),
        "L0_attn_Wk": (rng.standard_normal((D, D), dtype=np.float32) / 64),
        "L0_attn_bk": (rng.standard_normal((D,), dtype=np.float32) / 64),
        "L0_attn_Wv": (rng.standard_normal((D, D), dtype=np.float32) / 64),
        "L0_attn_bv": (rng.standard_normal((D,), dtype=np.float32) / 64),
        "L0_attn_Wo": (rng.standard_normal((D, D), dtype=np.float32) / 64),
        "L0_attn_bo": (rng.standard_normal((D,), dtype=np.float32) / 64),
        "L0_ffn_A": (rng.standard_normal((NEX, LR, D), dtype=np.float32) / 64),
        "L0_ffn_B": (rng.standard_normal((NEX, D, LR), dtype=np.float32) * 0.02),
        "L0_ffn_Wr": (rng.standard_normal((NEX, D), dtype=np.float32) * 0.02),
    }
    out = kernel(**ins)
    print("out", out.shape, out.dtype, np.abs(out).mean())


# version 24
# speedup vs baseline: 1.0288x; 1.0288x over previous
import math
import numpy as np
from contextlib import ExitStack

import concourse.bass as bass
import concourse.mybir as mybir
from concourse.bass_utils import run_bass_kernel_spmd

F32 = mybir.dt.float32
BF16 = mybir.dt.bfloat16
AF = mybir.ActivationFunctionType
ALU = mybir.AluOpType

NCORES = 8
T = 2048
D = 4096
NCH = 32
NH = 32
HD = 128
NEX = 4
LR = 16
EPS = 1e-5
EXP_SCALE = 1.0 / math.sqrt(HD)
EXP_BIAS = -40.0
GATE_BIAS = -10.0

ENGS = ["gpsimd", "sync", "tensor", "scalar", "vector"]


class Track:
    def __init__(self, name):
        self.name = name
        self.n = 0
        self.h = None


class Buf:
    def __init__(self, name, rng=None):
        self.name = name
        self.w = None
        self.rd = {}
        self.dsem = None
        self.rng = rng


class Plan:
    def __init__(self, mk_sem):
        self.ranged = []
        self.ops = {e: [] for e in ENGS}
        self.etr = {e: Track("t_" + e) for e in ENGS}
        self.dsems = []
        self.mk_sem = mk_sem
        self.waited = {e: {} for e in ENGS}

    def all_tracks(self):
        return list(self.etr.values()) + self.dsems

    def get_dsem(self, buf):
        if buf.dsem is None:
            tr = Track("d_" + buf.name)
            tr.h = self.mk_sem(tr.name)
            buf.dsem = tr
            self.dsems.append(tr)
        return buf.dsem

    def _wait(self, eng, tr, n):
        if n <= 0 or self.waited[eng].get(tr.name, 0) >= n:
            return
        self.waited[eng][tr.name] = n
        self.ops[eng].append(lambda e, h=tr, m=n: e.wait_ge(h.h, m))

    def step(self, eng, fns, reads=(), writes=(), dma=False, dsem=None):
        for b in list(reads) + list(writes):
            if b.w is not None:
                self._wait(eng, b.w[0], b.w[1])
            if b.rng is not None:
                for ob in self.ranged:
                    if ob is b or ob.rng[1] <= b.rng[0] or ob.rng[0] >= b.rng[1]:
                        continue
                    if ob.w is not None:
                        self._wait(eng, ob.w[0], ob.w[1])
                    if b in writes:
                        for tr, n in ob.rd.values():
                            self._wait(eng, tr, n)
        for b in writes:
            for tr, n in b.rd.values():
                self._wait(eng, tr, n)
        if not isinstance(fns, (list, tuple)):
            fns = [fns]
        if dma:
            assert dsem is not None, "dma step needs dsem buf"
            tr = self.get_dsem(dsem)
        else:
            tr = self.etr[eng]
        amt = 16 if dma else 1
        tr.n += amt
        fns = list(fns)
        last = fns[-1]
        fns[-1] = lambda e, f=last, h=tr, a=amt: f(e).then_inc(h.h, a)
        self.ops[eng].extend(fns)
        cert = (tr, tr.n)
        for b in writes:
            b.w = cert
            b.rd = {}
        for b in reads:
            b.rd[tr.name] = cert

    def barrier(self):
        snap = [(t, t.n) for t in self.all_tracks()]
        for e in ENGS:
            for t, n in snap:
                if t is self.etr[e]:
                    continue
                self._wait(e, t, n)

    def emit(self, block):
        for name in ENGS:
            fns = self.ops[name]
            if not fns:
                continue

            def body(e, fns=fns):
                for f in fns:
                    f(e)

            getattr(block, name)(body)


def build_nc():
    nc = bass.Bass("TRN2", target_bir_lowering=False)

    xh = nc.dram_tensor("xh", [128, NCH, T], F32, kind="ExternalInput").ap()
    wqkv = nc.dram_tensor("wqkv", [NH, 3, 128, NCH, 128], F32, kind="ExternalInput").ap()
    wo_a = nc.dram_tensor("wo_a", [NCH, 128, NH, 128], F32, kind="ExternalInput").ap()
    a_a = nc.dram_tensor("a_a", [128, NCH, NEX * LR], F32, kind="ExternalInput").ap()
    b_a = nc.dram_tensor("b_a", [NEX * LR, D], F32, kind="ExternalInput").ap()
    wr_a = nc.dram_tensor("wr_a", [128, NCH, NEX], F32, kind="ExternalInput").ap()
    bias_a = nc.dram_tensor("bias_a", [NH, 3 * 128], F32, kind="ExternalInput").ap()
    bo_a = nc.dram_tensor("bo_a", [NCH, 128], F32, kind="ExternalInput").ap()
    eye_a = nc.dram_tensor("eye_a", [128, 128], F32, kind="ExternalInput").ap()
    mpad_a = nc.dram_tensor("mpad_a", [128, 1152], F32, kind="ExternalInput").ap()
    ones_a = nc.dram_tensor("ones_a", [1, 512], F32, kind="ExternalInput").ap()
    e_a = nc.dram_tensor("e_a", [NEX, NEX * LR], F32, kind="ExternalInput").ap()

    out_hbm = nc.dram_tensor("out_hbm", [128, NCH, T], F32, kind="ExternalOutput").ap()
    dbg_hbm = nc.dram_tensor("dbg_hbm", [128, NCH * T], BF16, kind="ExternalOutput").ap()
    av_hbm = nc.dram_tensor("av_hbm", [4, 128, NH, 512], BF16, kind="Internal").ap()
    wo_hbm = nc.dram_tensor("wo_hbm", [NCH, 128, NH, 128], BF16, kind="Internal").ap()

    st = ExitStack()
    E_ = st.enter_context

    RT = E_(nc.sbuf_tensor("RT", [128, NCH * T], BF16))
    BIG = E_(nc.sbuf_tensor("BIG", [128, 28672], BF16))
    Rb = E_(nc.sbuf_tensor("Rb", [128, T], BF16))
    rbsb = E_(nc.sbuf_tensor("rbsb", [128, 512], BF16))
    lnrow2 = E_(nc.sbuf_tensor("lnrow2", [2, T], BF16))
    mbf = lnrow2[0:1, :]
    stg = E_(nc.sbuf_tensor("stg", [1, 512], BF16))
    scr0 = E_(nc.sbuf_tensor("scr0", [1, 512], F32))
    scr1 = E_(nc.sbuf_tensor("scr1", [1, 512], F32))
    nrec = E_(nc.sbuf_tensor("nrec", [1, 512], BF16))
    srtmp = E_(nc.sbuf_tensor("srtmp", [1, 512], F32))
    augsb = [E_(nc.sbuf_tensor(f"augsb{i}", [2, 384], BF16)) for i in range(2)]
    augwr = E_(nc.sbuf_tensor("augwr", [1, NEX], BF16))
    auga = E_(nc.sbuf_tensor("auga", [1, NEX * LR], BF16))
    boj = [E_(nc.sbuf_tensor(f"boj{i}", [1, 128], BF16)) for i in range(2)]
    ident = E_(nc.sbuf_tensor("ident", [128, 128], BF16))
    mpad = E_(nc.sbuf_tensor("mpad", [128, 1152], BF16))
    ones_f = E_(nc.sbuf_tensor("ones_f", [1, 128], F32))
    ones_b = E_(nc.sbuf_tensor("ones_b", [1, 512], BF16))
    ones_c = E_(nc.sbuf_tensor("ones_c", [128, 1], BF16))
    e_mat = E_(nc.sbuf_tensor("e_mat", [NEX, NEX * LR], BF16))
    ct = E_(nc.sbuf_tensor("ct", [128, 8], F32))

    psA = E_(nc.psum_tensor("psA", [128, 1024], F32))
    psB = E_(nc.psum_tensor("psB", [128, 1024], F32))
    psC = E_(nc.psum_tensor("psC", [128, 1024], F32))
    psE0 = E_(nc.psum_tensor("psE0", [128, 512], F32))
    psE1 = E_(nc.psum_tensor("psE1", [128, 512], F32))
    psAb = psA[:, :].bitcast(BF16)
    psCb = psC[:, :].bitcast(BF16)

    p = Plan(lambda name: E_(nc.semaphore(name)))
    for tr in p.etr.values():
        tr.h = E_(nc.semaphore(tr.name))

    B = lambda n: Buf(n)
    rt = B("rt")
    rtg = [B(f"rtg{i}") for i in range(8)]

    def rts_bufs(*cs):
        out = [rtg[c // 4] for c in cs]
        return out
    bA, bB = B("psA"), B("psB")
    bClo, bChi = B("psC_lo"), B("psC_hi")
    bE0, bE1 = B("psE0"), B("psE1")
    bRb, bmbf, bstg = B("Rb"), B("lnrow2"), B("stg")
    bscr0, bscr1, bnrec, brb = B("scr0"), B("scr1"), B("nrec"), B("rbsb")
    bsrt = B("srtmp")
    baugs = [B("augsb0"), B("augsb1")]
    baugwr, bauga = B("augwr"), B("auga")
    bboj = [B("boj0"), B("boj1")]
    cstD, cstM = B("cstD"), B("cstM")
    bavhbm, bwohbm = B("av_hbm"), B("wo_hbm")
    bwq, bwk, bwv = B("wq"), B("wk"), B("wv")
    bqT2 = [B("qT0"), B("qT1")]
    bkT2 = [B("kT0"), B("kT1")]
    bvH2 = [B("vH0"), B("vH1")]
    bvT = B("vT")
    bP = [B("P0"), B("P1")]
    bav = [B("av0"), B("av1")]
    basb, bwrsb = B("asb"), B("wrsb")
    bbsb = [B("bsb0"), B("bsb1")]
    bgmid, bgrb, blog, bgT = B("gmid"), B("grb"), B("logits"), B("gT")
    bgate = B("gatescr")
    bxsq = [B("xsq0"), B("xsq1")]
    boutst = [B("outst0"), B("outst1")]
    bavq, bwo = B("avq"), [B("wo0"), B("wo1")]
    bavqp = [B(f"avqp{i}") for i in range(4)]

    _ranges = {
        bwq: (0, 4096), bwk: (4096, 8192), bwv: (8192, 12288),
        bqT2[0]: (12288, 14336), bqT2[1]: (14336, 16384),
        bkT2[0]: (16384, 18432), bkT2[1]: (18432, 20480),
        bvH2[0]: (20480, 22528), bvH2[1]: (22528, 24576),
        bvT: (24576, 26624),
        bP[0]: (26624, 27136), bP[1]: (27136, 27648),
        bav[0]: (27648, 28160), bav[1]: (28160, 28672),
        bavqp[0]: (0, 4096), bavqp[1]: (4096, 8192),
        bavqp[2]: (8192, 12288), bavqp[3]: (12288, 16384),
        bwo[0]: (16384, 20480), bwo[1]: (20480, 24576),
        basb: (0, 2048), bwrsb: (2048, 2176),
        bbsb[0]: (2176, 3200), bbsb[1]: (3200, 4224),
        bgmid: (4224, 6272), bgrb: (6272, 8320),
        blog: (8320, 10368), bgT: (10368, 12416),
        bgate: (12416, 13056),
        bxsq[0]: (26624, 27136), bxsq[1]: (27136, 27648),
        boutst[0]: (14080, 16128), boutst[1]: (16128, 18176),
    }
    for _b, _r in _ranges.items():
        _b.rng = _r
        p.ranged.append(_b)
    bavqp2 = [B("avqp0a"), B("avqp0b")]

    wq_sb = BIG[:, 0:4096]
    wk_sb = BIG[:, 4096:8192]
    wv_sb = BIG[:, 8192:12288]
    qT2 = [BIG[:, 12288:14336], BIG[:, 14336:16384]]
    kT2 = [BIG[:, 16384:18432], BIG[:, 18432:20480]]
    vH2 = [BIG[:, 20480:22528], BIG[:, 22528:24576]]
    vT = BIG[:, 24576:26624]
    Psb = [BIG[:, 26624:27136], BIG[:, 27136:27648]]
    avsb = [BIG[:, 27648:28160], BIG[:, 28160:28672]]
    avq = BIG[:, 0:16384]
    wosb = [BIG[:, 16384:20480], BIG[:, 20480:24576]]
    a_sb = BIG[:, 0:2048]
    wr_sb = BIG[:, 2048:2176]
    b_sb = [BIG[0:64, 2176:3200], BIG[0:64, 3200:4224]]
    gmid = BIG[0:64, 4224:6272]
    grb = BIG[0:64, 6272:8320]
    logits = BIG[0:4, 8320:10368]
    gT_t = BIG[0:4, 10368:12416]
    Ltok = BIG[:, 12416:12480]
    c1t = BIG[:, 12480:12544]
    c2t = BIG[:, 12544:12608]
    c3t = BIG[:, 12608:12672]
    Gbf = BIG[:, 12672:12736]
    gf32 = BIG[:, 12736:12864].bitcast(F32)
    gf32b = BIG[:, 12864:12992].bitcast(F32)
    zt = BIG[:, 12992:13024].bitcast(F32)
    invt = BIG[:, 13024:13056].bitcast(F32)
    xsq = [BIG[:, 26624:27136], BIG[:, 27136:27648]]
    outst = [BIG[:, 14080 + 2048 * i: 14080 + 2048 * (i + 1)].bitcast(F32) for i in range(2)]

    ct_eps = ct[0:1, 0:1]
    ct_nh = ct[0:1, 1:2]
    ct_ph = ct[0:1, 2:3]
    ct_es = ct[:, 3:4]
    ct_eb = ct[:, 4:5]
    ct_gb = ct[:, 5:6]
    ct_m1 = ct[0:1, 6:7]

    RTs = lambda c, o, w: RT[:, T * c + o: T * c + o + w]


    def vmemsets(v):
        v.memset(ones_c[:, :], 1.0)
        v.memset(ct[:, 0:1], EPS)
        v.memset(ct[:, 1:2], -0.5)
        v.memset(ct[:, 2:3], 0.5)
        v.memset(ct[:, 3:4], EXP_SCALE)
        v.memset(ct[:, 4:5], EXP_BIAS)
        v.memset(ct[:, 5:6], GATE_BIAS)
        v.memset(Psb[0], 0.0)
        v.memset(Psb[1], 0.0)
        return v.memset(ct[:, 6:7], -1.0)

    p.step("vector", vmemsets, writes=[cstM, bP[0], bP[1]])

    for i in range(4):
        p.step("gpsimd",
               lambda g, i=i: g.dma_start(
                   RT[:, 16384 * i: 16384 * (i + 1)],
                   xh[:, 8 * i: 8 * (i + 1), :].rearrange("p c t -> p (c t)")),
               writes=[rt, rtg[2 * i], rtg[2 * i + 1]], dma=True, dsem=rtg[2 * i])
    p.step("gpsimd", lambda g: g.dma_start(ident[:, :], eye_a), writes=[cstD], dma=True, dsem=cstD)
    p.step("gpsimd", lambda g: g.dma_start(mpad[:, :], mpad_a), writes=[cstD], dma=True, dsem=cstD)
    p.step("gpsimd", lambda g: g.dma_start(ones_f[:, :], ones_a[:, 0:128]), writes=[cstD], dma=True, dsem=cstD)
    p.step("gpsimd", lambda g: g.dma_start(ones_b[:, :], ones_a), writes=[cstD], dma=True, dsem=cstD)
    p.step("gpsimd", lambda g: g.dma_start(e_mat[:, :], e_a), writes=[cstD], dma=True, dsem=cstD)
    p.step("gpsimd", lambda g: g.dma_start(augsb[0][1:2, :], bias_a[0:1, :]), writes=[baugs[0]], dma=True, dsem=baugs[0])
    p.step("gpsimd", lambda g: g.dma_start(augsb[1][1:2, :], bias_a[1:2, :]), writes=[baugs[1]], dma=True, dsem=baugs[1])

    def plan_stats_tq(tq):
        import os as _os
        nch = int(_os.environ.get("KMK_NCH", "32"))
        fin = int(_os.environ.get("KMK_FIN", "9"))
        if True:
            o = 512 * tq
            for c in range(nch):
                par = c % 2
                p.step("vector",
                       lambda v, c=c, o=o, par=par: v.tensor_tensor(
                           xsq[par], RTs(c, o, 512), RTs(c, o, 512), ALU.mult),
                       reads=rts_bufs(c), writes=[bxsq[par], bP[par]])

                def mms(t, c=c, o=o, par=par):
                    t.matmul(psC[0:1, 0:512], ones_c[:, :], RTs(c, o, 512),
                             start=(c == 0), stop=(c == nch - 1))
                    return t.matmul(psC[0:1, 512:1024], ones_c[:, :], xsq[par],
                                    start=(c == 0), stop=(c == nch - 1))
                p.step("tensor", mms, reads=rts_bufs(c) + [bxsq[par], cstM],
                       writes=[bClo, bChi])
            if not fin:
                return
            if fin >= 1:
                p.step("vector",
                       lambda v: v.tensor_scalar_mul(scr0[0:1, 0:512], psC[0:1, 0:512], 1.0 / D),
                       reads=[bClo], writes=[bscr0])
            if fin >= 2:
                p.step("vector",
                       lambda v, o=o: v.tensor_copy(mbf[0:1, o:o + 512], scr0[0:1, 0:512]),
                       reads=[bscr0], writes=[bmbf])
            if fin >= 3:
                p.step("vector",
                       lambda v: v.tensor_tensor(scr1[0:1, 0:512], scr0[0:1, 0:512],
                                                 scr0[0:1, 0:512], ALU.mult),
                       reads=[bscr0], writes=[bscr1])
            if fin >= 4:
                p.step("vector",
                       lambda v: v.scalar_tensor_tensor(scr0[0:1, 0:512], psC[0:1, 512:1024],
                                                        1.0 / D, scr1[0:1, 0:512],
                                                        ALU.mult, ALU.subtract),
                       reads=[bChi, bscr1], writes=[bscr0])
            if fin >= 5:
                p.step("scalar",
                       lambda s: s.activation(scr1[0:1, 0:512], scr0[0:1, 0:512], AF.Ln,
                                              bias=ct_eps),
                       reads=[bscr0, cstM], writes=[bscr1])
            if fin >= 6:
                p.step("scalar",
                       lambda s: s.activation(scr0[0:1, 0:512], scr1[0:1, 0:512], AF.Exp,
                                              scale=ct_nh),
                       reads=[bscr1, cstM], writes=[bscr0])
            if fin >= 7:
                p.step("scalar",
                       lambda s: s.activation(stg[0:1, 0:512], scr1[0:1, 0:512],
                                              AF.Exp, scale=ct_ph),
                       reads=[bscr1, cstM], writes=[bstg])
                p.step("gpsimd",
                       lambda g, o=o: g.dma_start(lnrow2[1:2, o:o + 512], stg[0:1, 0:512]),
                       reads=[bstg, bmbf], writes=[bmbf], dma=True, dsem=bmbf)
            if fin >= 8:
                p.step("tensor",
                       lambda t: t.matmul(psE0[:, :], ones_f[:, :], scr0[0:1, 0:512],
                                          start=True, stop=True),
                       reads=[bscr0, cstD], writes=[bE0])
            if fin >= 9:
                p.step("vector",
                       lambda v, o=o: v.tensor_copy(Rb[:, o:o + 512], psE0[:, :]),
                       reads=[bE0], writes=[bRb])

    def plan_stats_couter():
        sumv = [psA[0:1, 0:512], psA[0:1, 512:1024], psB[0:1, 0:512], psB[0:1, 512:1024]]
        ssqv = [psC[0:1, 0:512], psC[0:1, 512:1024], psE0[0:1, :], psE1[0:1, :]]
        sumb = [bA, bA, bB, bB]
        ssqb = [bClo, bChi, bE0, bE1]
        for c in range(NCH):
            for tq in range(4):
                o = 512 * tq
                par = (4 * c + tq) % 2
                p.step("vector",
                       lambda v, c=c, o=o, par=par: v.tensor_tensor(
                           xsq[par], RTs(c, o, 512), RTs(c, o, 512), ALU.mult),
                       reads=rts_bufs(c), writes=[bxsq[par], bP[par]])

                def mms(t, c=c, o=o, par=par, tq=tq):
                    t.matmul(sumv[tq], ones_c[:, :], RTs(c, o, 512),
                             start=(c == 0), stop=(c == NCH - 1))
                    return t.matmul(ssqv[tq], ones_c[:, :], xsq[par],
                                    start=(c == 0), stop=(c == NCH - 1))
                p.step("tensor", mms, reads=rts_bufs(c) + [bxsq[par], cstM],
                       writes=[sumb[tq], ssqb[tq]])
        for tq in range(4):
            o = 512 * tq
            p.step("vector",
                   lambda v, tq=tq: v.tensor_scalar_mul(scr0[0:1, 0:512], sumv[tq], 1.0 / D),
                   reads=[sumb[tq]], writes=[bscr0])
            p.step("vector",
                   lambda v, o=o: v.tensor_copy(mbf[0:1, o:o + 512], scr0[0:1, 0:512]),
                   reads=[bscr0], writes=[bmbf])
            p.step("vector",
                   lambda v: v.tensor_tensor(scr1[0:1, 0:512], scr0[0:1, 0:512],
                                             scr0[0:1, 0:512], ALU.mult),
                   reads=[bscr0], writes=[bscr1])
            p.step("vector",
                   lambda v, tq=tq: v.scalar_tensor_tensor(scr0[0:1, 0:512], ssqv[tq],
                                                           1.0 / D, scr1[0:1, 0:512],
                                                           ALU.mult, ALU.subtract),
                   reads=[ssqb[tq], bscr1], writes=[bscr0])
            p.step("scalar",
                   lambda s: s.activation(scr1[0:1, 0:512], scr0[0:1, 0:512], AF.Ln,
                                          bias=ct_eps),
                   reads=[bscr0, cstM], writes=[bscr1])
            p.step("scalar",
                   lambda s: s.activation(scr0[0:1, 0:512], scr1[0:1, 0:512], AF.Exp,
                                          scale=ct_nh),
                   reads=[bscr1, cstM], writes=[bscr0])
            p.step("scalar",
                   lambda s: s.activation(stg[0:1, 0:512], scr1[0:1, 0:512],
                                          AF.Exp, scale=ct_ph),
                   reads=[bscr1, cstM], writes=[bstg])
            p.step("gpsimd",
                   lambda g, o=o: g.dma_start(lnrow2[1:2, o:o + 512], stg[0:1, 0:512]),
                   reads=[bstg, bmbf], writes=[bmbf], dma=True, dsem=bmbf)
            rbv = [psC[:, 0:512], psC[:, 512:1024], psE0[:, :], psE1[:, :]][tq]
            rbb = [bClo, bChi, bE0, bE1][tq]
            p.step("tensor",
                   lambda t, rbv=rbv: t.matmul(rbv, ones_f[:, :], scr0[0:1, 0:512],
                                               start=True, stop=True),
                   reads=[bscr0, cstD], writes=[rbb])
            p.step("vector",
                   lambda v, o=o, rbv=rbv: v.tensor_copy(Rb[:, o:o + 512], rbv),
                   reads=[rbb], writes=[bRb])

    def plan_ffn(final, sub="all", skip_stats=False):
        order = ["dma", "stats", "spre", "router", "mid", "gates", "delta"]
        lim = order.index(sub) if sub in order else 99

        def upto(k):
            return order.index(k) <= lim
        p.step("gpsimd", lambda g: g.dma_start(a_sb, a_a.rearrange("p c j -> p (c j)")),
               writes=[basb], dma=True, dsem=basb)
        p.step("gpsimd", lambda g: g.dma_start(wr_sb, wr_a.rearrange("p c j -> p (c j)")),
               writes=[bwrsb], dma=True, dsem=bwrsb)
        p.step("gpsimd", lambda g: g.dma_start(b_sb[0], b_a[:, 0:1024]),
               writes=[bbsb[0]], dma=True, dsem=bbsb[0])

        if not upto("stats"):
            return
        if not skip_stats:
            if not final:
                plan_stats_couter()
            else:
                for _tq in range(4):
                    plan_stats_tq(_tq)
        if not upto("spre"):
            return

        def swr(t):
            mm = None
            for c in range(NCH):
                mm = t.matmul(psA[0:1, 0:NEX], ones_c[:, :], wr_sb[:, NEX * c:NEX * (c + 1)],
                              start=(c == 0), stop=(c == NCH - 1))
            return mm
        p.step("tensor", swr, reads=[bwrsb, cstM], writes=[bA])

        def sa(t):
            mm = None
            for c in range(NCH):
                mm = t.matmul(psB[0:1, 0:64], ones_c[:, :], a_sb[:, 64 * c:64 * (c + 1)],
                              start=(c == 0), stop=(c == NCH - 1))
            return mm
        p.step("tensor", sa, reads=[basb, cstM], writes=[bB])
        p.step("vector", lambda v: v.tensor_scalar_mul(augwr[:, :], psA[0:1, 0:NEX], -1.0),
               reads=[bA], writes=[baugwr])
        p.step("vector", lambda v: v.tensor_scalar_mul(auga[:, :], psB[0:1, 0:64], -1.0),
               reads=[bB], writes=[bauga])

        if not upto("router"):
            return
        for tq in range(4):
            o = 512 * tq

            def rmm(t, o=o):
                for c in range(NCH):
                    t.matmul(psE0[0:4, :], wr_sb[:, NEX * c:NEX * (c + 1)],
                             RTs(c, o, 512), start=(c == 0), stop=False)
                return t.matmul(psE0[0:4, :], augwr[:, :], mbf[0:1, o:o + 512],
                                start=False, stop=True)
            p.step("tensor", rmm, reads=[rt, bwrsb, baugwr, bmbf] + rtg, writes=[bE0])
            p.step("vector",
                   lambda v, o=o: v.tensor_tensor(logits[:, o:o + 512], psE0[0:4, :],
                                                  Rb[0:4, o:o + 512], ALU.mult),
                   reads=[bE0, bRb], writes=[blog])

        if not upto("mid"):
            return
        if not upto("gates"):
            return
        def ltr(t):
            mm = None
            for i in range(16):
                mm = t.transpose(psCb[:, 4 * i:4 * (i + 1)], logits[:, 128 * i:128 * (i + 1)],
                                 ident[0:4, 0:4])
            return mm
        p.step("tensor", ltr, reads=[blog, cstD], writes=[bClo, bChi])
        p.step("vector", lambda v: v.tensor_copy(Ltok, psCb[:, 0:64]),
               reads=[bClo, bChi], writes=[bgate])

        for half, (ps, bps) in enumerate([(psA, bA), (psB, bB)]):
            ho = 1024 * half

            def midmm(t, ho=ho, ps=ps):
                for c in range(NCH):
                    t.matmul(ps[0:64, 0:512], a_sb[:, 64 * c:64 * (c + 1)],
                             RTs(c, ho, 512), start=(c == 0), stop=False)
                    t.matmul(ps[0:64, 512:1024], a_sb[:, 64 * c:64 * (c + 1)],
                             RTs(c, ho + 512, 512), start=(c == 0), stop=False)
                t.matmul(ps[0:64, 0:512], auga[:, :], mbf[0:1, ho:ho + 512],
                         start=False, stop=True)
                return t.matmul(ps[0:64, 512:1024], auga[:, :], mbf[0:1, ho + 512:ho + 1024],
                                start=False, stop=True)
            p.step("tensor", midmm, reads=[rt, basb, bauga, bmbf] + rtg, writes=[bps])

        lv = lambda ap, e: ap.rearrange("p (i e) -> p i e", e=4)[:, :, e]
        gv = lambda e: gf32b.rearrange("p (i e) -> p i e", e=4)[:, :, e]

        def cmps(v):
            last = None
            for e in range(4):
                tgts = [c1t, c2t, c3t]
                k = 0
                for j in range(4):
                    if j == e:
                        continue
                    op = ALU.is_ge if j < e else ALU.is_gt
                    last = v.tensor_tensor(lv(tgts[k], e), lv(Ltok, j), lv(Ltok, e), op)
                    k += 1
            return last
        p.step("vector", cmps, reads=[bgate], writes=[bgate])
        p.step("vector", lambda v: v.tensor_tensor(c1t, c1t, c2t, ALU.add),
               reads=[bgate], writes=[bgate])
        p.step("vector", lambda v: v.tensor_tensor(c1t, c1t, c3t, ALU.add),
               reads=[bgate], writes=[bgate])
        p.step("vector", lambda v: v.tensor_scalar(c2t, c1t, 1.5, None, ALU.is_le),
               reads=[bgate], writes=[bgate])
        p.step("scalar",
               lambda s: s.activation(gf32[:, :], Ltok, AF.Exp, bias=ct_gb),
               reads=[bgate, cstM], writes=[bgate])
        p.step("vector", lambda v: v.tensor_tensor(gf32b[:, :], gf32[:, :], c2t, ALU.mult),
               reads=[bgate], writes=[bgate])
        p.step("vector", lambda v: v.tensor_tensor(zt[:, :], gv(0), gv(1), ALU.add),
               reads=[bgate], writes=[bgate])
        p.step("vector", lambda v: v.tensor_tensor(zt[:, :], zt[:, :], gv(2), ALU.add),
               reads=[bgate], writes=[bgate])
        p.step("vector", lambda v: v.tensor_tensor(zt[:, :], zt[:, :], gv(3), ALU.add),
               reads=[bgate], writes=[bgate])
        p.step("vector", lambda v: v.reciprocal(invt[:, :], zt[:, :]),
               reads=[bgate], writes=[bgate])

        def gmuls(v):
            last = None
            for e in range(4):
                last = v.tensor_tensor(lv(Gbf, e), gv(e), invt[:, :], ALU.mult)
            return last
        p.step("vector", gmuls, reads=[bgate], writes=[bgate])

        def gtr(t):
            mm = None
            for i in range(16):
                mm = t.transpose(psCb[0:4, 128 * i:128 * (i + 1)], Gbf[:, 4 * i:4 * (i + 1)],
                                 ident[:, :])
            return mm
        p.step("tensor", gtr, reads=[bgate, cstD], writes=[bClo, bChi])
        p.step("vector", lambda v: v.tensor_copy(gT_t[:, 0:1024], psCb[0:4, 0:1024]),
               reads=[bClo, bChi], writes=[bgT])
        p.step("vector", lambda v: v.tensor_copy(gT_t[:, 1024:2048], psCb[0:4, 1024:2048]),
               reads=[bClo, bChi], writes=[bgT])

        for half, bmid in enumerate([bA, bB]):
            ho = 1024 * half
            psM = psA if half == 0 else psB

            def emm(t, ho=ho):
                t.matmul(psC[0:64, 0:512], e_mat[:, :], gT_t[:, ho:ho + 512],
                         start=True, stop=True)
                return t.matmul(psC[0:64, 512:1024], e_mat[:, :], gT_t[:, ho + 512:ho + 1024],
                                start=True, stop=True)
            p.step("tensor", emm, reads=[bgT, cstD], writes=[bClo, bChi])
            p.step("vector",
                   lambda v, ho=ho: v.tensor_tensor(grb[:, ho:ho + 1024], psC[0:64, 0:1024],
                                                    Rb[0:64, ho:ho + 1024], ALU.mult),
                   reads=[bClo, bChi, bRb], writes=[bgrb])
            p.step("vector",
                   lambda v, ho=ho, psM=psM: v.tensor_tensor(
                       gmid[:, ho:ho + 1024], psM[0:64, 0:1024], grb[:, ho:ho + 1024],
                       ALU.mult),
                   reads=[bmid, bgrb], writes=[bgmid])

        if not upto("delta"):
            return
        for g in range(4):
            if g < 3:
                p.step("gpsimd",
                       lambda gg, g=g: gg.dma_start(b_sb[(g + 1) % 2],
                                                    b_a[:, 1024 * (g + 1):1024 * (g + 2)]),
                       writes=[bbsb[(g + 1) % 2]], dma=True, dsem=bbsb[(g + 1) % 2])
            for j in range(8 * g, 8 * g + 8):
                jo = 128 * (j % 8)

                def dmm(t, j=j, jo=jo, g=g):
                    bs = b_sb[g % 2]
                    t.matmul(psA[:, 0:512], bs[:, jo:jo + 128], gmid[:, 0:512],
                             start=True, stop=True)
                    t.matmul(psA[:, 512:1024], bs[:, jo:jo + 128], gmid[:, 512:1024],
                             start=True, stop=True)
                    t.matmul(psB[:, 0:512], bs[:, jo:jo + 128], gmid[:, 1024:1536],
                             start=True, stop=True)
                    return t.matmul(psB[:, 512:1024], bs[:, jo:jo + 128], gmid[:, 1536:2048],
                                    start=True, stop=True)
                p.step("tensor", dmm, reads=[bbsb[g % 2], bgmid], writes=[bA, bB])
                if not final:
                    p.step("vector",
                           lambda v, j=j: v.tensor_tensor(RTs(j, 0, 1024), RTs(j, 0, 1024),
                                                          psA[:, :], ALU.add),
                           reads=[bA], writes=[rt] + rts_bufs(j))
                    p.step("vector",
                           lambda v, j=j: v.tensor_tensor(RTs(j, 1024, 1024),
                                                          RTs(j, 1024, 1024),
                                                          psB[:, :], ALU.add),
                           reads=[bB], writes=[rt] + rts_bufs(j))
                else:
                    for half, bp in enumerate([bA, bB]):
                        par = (j + half) % 2
                        ps = psA if half == 0 else psB
                        p.step("vector",
                               lambda v, j=j, half=half, par=par, ps=ps: v.tensor_tensor(
                                   outst[par][:, 0:1024], RTs(j, 1024 * half, 1024),
                                   ps[:, :], ALU.add),
                               reads=[bp, rt] + rts_bufs(j), writes=[boutst[par]])
                        p.step("sync",
                               lambda y, j=j, half=half, par=par: y.dma_start(
                                   out_hbm[:, j, 1024 * half:1024 * (half + 1)],
                                   outst[par][:, 0:1024]),
                               reads=[boutst[par]], dma=True, dsem=boutst[par])

    def plan_head(h):
        par_h = h % 2
        qT = qT2[par_h]
        kT = kT2[par_h]
        vH = vH2[par_h]
        bqT, bkT, bvH = bqT2[par_h], bkT2[par_h], bvH2[par_h]
        proj_t = []
        projs = [
            ("q", wq_sb, bwq, qT, bqT, 0),
            ("k", wk_sb, bwk, kT, bkT, 128),
            ("v", wv_sb, bwv, vT, bvT, 256),
        ]
        for pi, (nm, wsb, bw, oT, boT, so) in enumerate(projs):
            ppar = pi % 2
            def srow(t, wsb=wsb):
                mm = None
                for g in range(8):
                    mm = t.matmul(psC[0:1, 512:1024], ones_c[:, :],
                                  wsb[:, 512 * g:512 * (g + 1)],
                                  start=(g == 0), stop=(g == 7))
                return mm
            proj_t.append(lambda srow=srow, bw=bw: p.step(
                "tensor", srow, reads=[bw, cstM], writes=[bChi]))
            proj_t.append(lambda: p.step(
                "vector",
                lambda v: v.tensor_copy(srtmp[0:1, 0:512], psC[0:1, 512:1024]),
                reads=[bChi], writes=[bsrt]))
            proj_t.append(lambda: p.step(
                "vector",
                lambda v: v.tensor_tensor(srtmp[0:1, 0:256], srtmp[0:1, 0:256],
                                          srtmp[0:1, 256:512], ALU.add),
                reads=[bsrt], writes=[bsrt]))
            proj_t.append(lambda so=so: p.step(
                "vector",
                lambda v: v.scalar_tensor_tensor(augsb[par_h][0:1, so:so + 128],
                                                 srtmp[0:1, 0:128], -1.0,
                                                 srtmp[0:1, 128:256],
                                                 ALU.mult, ALU.subtract),
                reads=[bsrt], writes=[baugs[par_h]]))
            for c in range(NCH):
                def cmm(t, c=c, wsb=wsb):
                    wch = wsb[:, 128 * c:128 * (c + 1)]
                    t.matmul(psA[:, 0:512], wch, RTs(c, 0, 512), start=(c == 0), stop=False)
                    t.matmul(psA[:, 512:1024], wch, RTs(c, 512, 512), start=(c == 0), stop=False)
                    t.matmul(psB[:, 0:512], wch, RTs(c, 1024, 512), start=(c == 0), stop=False)
                    return t.matmul(psB[:, 512:1024], wch, RTs(c, 1536, 512),
                                    start=(c == 0), stop=False)
                proj_t.append(lambda c=c, cmm=cmm, bw=bw: p.step(
                    "tensor", cmm, reads=rts_bufs(c) + [rt, bw], writes=[bA, bB]))
            if h + 1 < NH:
                proj_t.append(lambda pi=pi, wsb=wsb, bw=bw: p.step(
                    "gpsimd",
                    lambda g: g.dma_start(wsb, wqkv[h + 1, pi].rearrange("p c j -> p (c j)")),
                    writes=[bw], dma=True, dsem=bw))

            def augmm(t, so=so):
                ab = augsb[par_h]
                mm = None
                for half, ps in enumerate([psA, psB]):
                    ho = 1024 * half
                    t.matmul(ps[:, 0:512], ab[0:2, so:so + 128], lnrow2[0:2, ho:ho + 512],
                             start=False, stop=True)
                    mm = t.matmul(ps[:, 512:1024], ab[0:2, so:so + 128],
                                  lnrow2[0:2, ho + 512:ho + 1024], start=False, stop=True)
                return mm
            proj_t.append(lambda augmm=augmm: p.step(
                "tensor", augmm, reads=[baugs[par_h], bmbf], writes=[bA, bB]))
            for half, (ps, bp) in enumerate([(psA, bA), (psB, bB)]):
                ho = 1024 * half
                proj_t.append(lambda ps=ps, bp=bp, ho=ho, oT=oT, boT=boT: p.step(
                    "vector",
                    lambda v: v.tensor_tensor(oT[:, ho:ho + 1024], ps[:, :],
                                              Rb[:, ho:ho + 1024], ALU.mult),
                    reads=[bp, bRb], writes=[boT]))
        if h + 1 < NH:
            proj_t.append(lambda: p.step(
                "gpsimd",
                lambda g: g.dma_start(augsb[(h + 1) % 2][1:2, :], bias_a[h + 1:h + 2, :]),
                writes=[baugs[(h + 1) % 2]], dma=True, dsem=baugs[(h + 1) % 2]))
        proj_t.append(lambda: p.step(
            "gpsimd",
            lambda g: g.dma_start(wo_hbm[h].rearrange("b c d -> b (c d)"),
                                  wo_a[h].rearrange("b c d -> b (c d)")),
            writes=[bwohbm], dma=True, dsem=bwohbm))

        def vtr(t):
            mm = None
            for i in range(16):
                mm = t.transpose(psAb[:, 128 * i:128 * (i + 1)],
                                 vT[:, 128 * i:128 * (i + 1)], ident[:, :])
            return mm
        proj_t.append(lambda: p.step("tensor", vtr, reads=[bvT, cstD], writes=[bA]))
        proj_t.append(lambda vH=vH, bvH=bvH: p.step(
            "vector", lambda v: v.tensor_copy(vH[:, :], psAb[:, :]),
            reads=[bA], writes=[bvH]))

        attn_t = []

        def smm_step(m4, c, qT=qT, kT=kT, bqT=bqT, bkT=bkT):
            j0 = max(0, c - 4 * m4)
            w = 512 - 128 * j0

            def smm(t, m4=m4, c=c, w=w):
                return t.matmul(psC[:, 0:w], kT[:, 128 * c:128 * (c + 1)],
                                qT[:, 512 * m4 + 512 - w: 512 * (m4 + 1)],
                                start=True, stop=True)
            p.step("tensor", smm, reads=[bkT, bqT], writes=[bClo])

        for m4 in range(4):
            nch = 4 * (m4 + 1)
            if m4 == 0:
                attn_t.append(lambda: smm_step(0, 0))
            for c in range(nch):
                par = c % 2
                j0 = max(0, c - 4 * m4)
                rs = 128 * j0
                diag = c >= 4 * m4

                w = 512 - rs

                def expst(par=par, w=w, rs=rs):
                    p.step("scalar",
                           lambda s: s.activation(Psb[par][:, rs:512], psC[:, 0:w],
                                                  AF.Exp, bias=ct_eb, scale=ct_es),
                           reads=[bClo, cstM], writes=[bP[par]])
                attn_t.append(expst)
                if diag:
                    mw = rs + 128

                    def maskst(par=par, mw=mw):
                        p.step("vector",
                               lambda v: v.tensor_tensor(Psb[par][:, 0:mw],
                                                         Psb[par][:, 0:mw],
                                                         mpad[:, 1152 - mw:1152], ALU.mult),
                               reads=[cstD], writes=[bP[par]])
                    attn_t.append(maskst)
                if c + 1 < nch:
                    attn_t.append(lambda m4=m4, c=c: smm_step(m4, c + 1))

                cpair = ([c - 1, c] if c % 2 == 1 else [c]) \
                    if (c % 2 == 1 or c == nch - 1) else None

                def savst(c=c, par=par, m4=m4, nch=nch, cpair=cpair, vH=vH, bvH=bvH):
                    reads = [bP[par], bvH]
                    writes = [bE0]
                    if cpair is not None:
                        reads += [bP[cc % 2] for cc in cpair] + [cstM]
                        writes += [bE1]

                    def sav(t):
                        mm = t.matmul(psE0[:, :], vH[:, 128 * c:128 * (c + 1)],
                                      Psb[par][:, :], start=(c == 0), stop=(c == nch - 1))
                        if cpair is not None:
                            for cc in cpair:
                                mm = t.matmul(psE1[0:1, :], ones_c[:, :], Psb[cc % 2][:, :],
                                              start=(cc == 0), stop=(cc == nch - 1))
                        return mm
                    p.step("tensor", sav, reads=reads, writes=writes)
                attn_t.append(savst)

            if m4 < 3:
                attn_t.append(lambda m4=m4: smm_step(m4 + 1, 0))

            def tailst(m4=m4):
                p.step("scalar",
                       lambda s: s.activation(scr0[0:1, 0:512], psE1[0:1, :], AF.Ln),
                       reads=[bE1], writes=[bscr0])
                p.step("scalar",
                       lambda s: s.activation(nrec[0:1, 0:512], scr0[0:1, 0:512], AF.Exp,
                                              scale=ct_m1),
                       reads=[bscr0, cstM], writes=[bnrec])
                p.step("tensor",
                       lambda t: t.matmul(psE1[:, :], ones_b[0:1, 0:128], nrec[0:1, 0:512],
                                          start=True, stop=True),
                       reads=[bnrec, cstD], writes=[bE1])
                p.step("vector", lambda v: v.tensor_copy(rbsb[:, :], psE1[:, :]),
                       reads=[bE1], writes=[brb])
                par = (h * 4 + m4) % 2
                p.step("vector",
                       lambda v, par=par: v.tensor_tensor(avsb[par][:, :], psE0[:, :],
                                                          rbsb[:, :], ALU.mult),
                       reads=[bE0, brb], writes=[bav[par]])
                p.step("sync",
                       lambda y, par=par: y.dma_start(av_hbm[m4, :, h, :], avsb[par][:, :]),
                       reads=[bav[par]], writes=[bavhbm], dma=True, dsem=bav[par])
            attn_t.append(tailst)
        return proj_t, attn_t

    def emit_interleaved(a, b):
        if not b:
            for f in a:
                f()
            return
        ratio = len(a) / len(b)
        bi = 0.0
        j = 0
        for i, f in enumerate(a):
            f()
            while (i + 1) / ratio >= j + 1 and j < len(b):
                b[j]()
                j += 1
        while j < len(b):
            b[j]()
            j += 1

    def plan_oproj():
        def load_avq_piece(tq, pc):
            p.step("gpsimd",
                   lambda g, tq=tq, pc=pc: g.dma_start(
                       avq[:, 4096 * pc:4096 * (pc + 1)],
                       av_hbm[tq, :, 8 * pc:8 * (pc + 1), :].rearrange("p h j -> p (h j)")),
                   reads=[bavhbm], writes=[bavqp[pc]], dma=True, dsem=bavqp[pc])

        for tq in range(4):
            for pc in range(4):
                load_avq_piece(tq, pc)
            if tq == 0:
                p.step("gpsimd",
                       lambda g: g.dma_start(
                           wosb[0], wo_hbm[0].rearrange("p h j -> p (h j)")),
                       reads=[bwohbm], writes=[bwo[0]], dma=True, dsem=bwo[0])
                p.step("gpsimd", lambda g: g.dma_start(boj[0][:, :], bo_a[0:1, :]),
                       writes=[bboj[0]], dma=True, dsem=bboj[0])
            js = list(range(NCH)) if tq % 2 == 0 else list(range(NCH - 1, -1, -1))
            for ji, j in enumerate(js):
                par = j % 2
                ps = psA if par == 0 else psB
                bp = bA if par == 0 else bB
                if ji + 1 < NCH:
                    jn = js[ji + 1]
                    p.step("gpsimd",
                           lambda g, jn=jn: g.dma_start(
                               wosb[jn % 2], wo_hbm[jn].rearrange("p h j -> p (h j)")),
                           reads=[bwohbm], writes=[bwo[jn % 2]], dma=True, dsem=bwo[jn % 2])
                    p.step("gpsimd",
                           lambda g, jn=jn: g.dma_start(boj[jn % 2][:, :],
                                                        bo_a[jn:jn + 1, :]),
                           writes=[bboj[jn % 2]], dma=True, dsem=bboj[jn % 2])

                for pc in range(4):
                    def ommp(t, j=j, par=par, ps=ps, pc=pc):
                        w = wosb[par]
                        mm = None
                        for hh in range(8 * pc, 8 * (pc + 1)):
                            mm = t.matmul(ps[:, 0:512], w[:, 128 * hh:128 * (hh + 1)],
                                          avq[:, 512 * hh:512 * (hh + 1)],
                                          start=(hh == 0), stop=False)
                        return mm
                    p.step("tensor", ommp, reads=[bavqp[pc], bwo[par], cstD], writes=[bp])
                p.step("tensor",
                       lambda t, par=par, ps=ps: t.matmul(ps[:, 0:512], boj[par][:, :],
                                                          ones_b[:, :], start=False, stop=True),
                       reads=[bboj[par], cstD], writes=[bp])
                p.step("vector",
                       lambda v, j=j, tq=tq, ps=ps: v.tensor_tensor(
                           RTs(j, 512 * tq, 512), RTs(j, 512 * tq, 512), ps[:, 0:512],
                           ALU.add),
                       reads=[bp], writes=[rt] + rts_bufs(j))
            plan_stats_tq(tq)

    def dump_rt():
        p.step("sync", lambda y: y.dma_start(dbg_hbm, RT[:, :]), reads=[rt],
               dma=True, dsem=rt)

    import os
    stage = int(os.environ.get("KMK_STAGE", "6"))
    sub = os.environ.get("KMK_SUB", "all")
    dbg = int(os.environ.get("KMK_DBG", "0"))
    plan_ffn(final=False, sub=sub)
    if dbg == 1:
        dump_rt()
    if stage >= 2:
        for _tq in range(4):
            plan_stats_tq(_tq)
    if stage >= 3:
        for pi, (wsb, wb) in enumerate([(wq_sb, bwq), (wk_sb, bwk), (wv_sb, bwv)]):
            p.step("gpsimd",
                   lambda g, pi=pi, wsb=wsb: g.dma_start(
                       wsb, wqkv[0, pi].rearrange("p c j -> p (c j)")),
                   writes=[wb], dma=True, dsem=wb)
        nh = 1 if stage == 3 else NH
        prev_attn = []
        for h in range(nh):
            proj_t, attn_t = plan_head(h)
            emit_interleaved(proj_t, prev_attn)
            prev_attn = attn_t
        for f in prev_attn:
            f()
    if stage >= 5:
        plan_oproj()
        if dbg == 2:
            dump_rt()
    if stage >= 6:
        plan_ffn(final=True, skip_stats=True)

    for tr in p.dsems:
        p.ops["sync"].append(lambda e, h=tr, n=tr.n: e.wait_ge(h.h, n))

    block = E_(nc.Block())
    p.emit(block)
    st.close()
    return nc


def _prep(inputs):
    f = lambda k: np.asarray(inputs[k], np.float32)
    x = f("L0_attn_x")
    Wq, Wk, Wv, Wo = f("L0_attn_Wq"), f("L0_attn_Wk"), f("L0_attn_Wv"), f("L0_attn_Wo")
    bq, bk, bv, bo = f("L0_attn_bq"), f("L0_attn_bk"), f("L0_attn_bv"), f("L0_attn_bo")
    A, Bm, Wr = f("L0_ffn_A"), f("L0_ffn_B"), f("L0_ffn_Wr")

    def wblocks(W):
        WT = np.ascontiguousarray(W.T)
        return np.ascontiguousarray(
            WT.reshape(NCH, 128, NH, 128).transpose(2, 1, 0, 3))

    wq_b, wk_b, wv_b = wblocks(Wq), wblocks(Wk), wblocks(Wv)
    wqkv = np.ascontiguousarray(np.stack([wq_b, wk_b, wv_b], axis=1))
    WoT = np.ascontiguousarray(Wo.T)
    wo_a = np.ascontiguousarray(
        WoT.reshape(NH, 128, NCH, 128).transpose(2, 1, 0, 3))
    a_a = np.ascontiguousarray(
        A.transpose(2, 0, 1).reshape(D, NEX * LR).reshape(NCH, 128, NEX * LR)
        .transpose(1, 0, 2))
    b_a = np.ascontiguousarray(Bm.transpose(0, 2, 1).reshape(NEX * LR, D))
    wr_a = np.ascontiguousarray(
        Wr.T.reshape(NCH, 128, NEX).transpose(1, 0, 2))
    bias_a = np.ascontiguousarray(
        np.stack([bq.reshape(NH, 128), bk.reshape(NH, 128), bv.reshape(NH, 128)],
                 axis=1).reshape(NH, 3 * 128))
    bo_a = np.ascontiguousarray(bo.reshape(NCH, 128))
    eye_a = np.eye(128, dtype=np.float32)
    kk = np.arange(128)[:, None]
    col = np.arange(1152)[None, :]
    mpad_a = ((col - 1024) >= kk).astype(np.float32)
    ones_a = np.ones((1, 512), np.float32)
    e_a = np.zeros((NEX, NEX * LR), np.float32)
    for e in range(NEX):
        e_a[e, LR * e:LR * (e + 1)] = 2.0

    shared = dict(wqkv=wqkv, wo_a=wo_a, a_a=a_a, b_a=b_a, wr_a=wr_a,
                  bias_a=bias_a, bo_a=bo_a, eye_a=eye_a, mpad_a=mpad_a,
                  ones_a=ones_a, e_a=e_a)
    in_maps = []
    for b in range(NCORES):
        xh = np.ascontiguousarray(
            x[b].T.reshape(NCH, 128, T).transpose(1, 0, 2))
        m = dict(shared)
        m["xh"] = xh
        in_maps.append(m)
    return in_maps


def kernel(**inputs):
    in_maps = _prep(inputs)
    nc = build_nc()
    res = run_bass_kernel_spmd(nc, in_maps, core_ids=list(range(NCORES)))
    outs = []
    for r in res.results:
        o = r["out_hbm"]
        outs.append(np.ascontiguousarray(o.transpose(1, 0, 2).reshape(D, T).T))
    return np.stack(outs).astype(np.float32)


if __name__ == "__main__":
    rng = np.random.default_rng(0)
    ins = {
        "L0_attn_x": rng.standard_normal((8, T, D), dtype=np.float32),
        "L0_attn_Wq": (rng.standard_normal((D, D), dtype=np.float32) / 64),
        "L0_attn_bq": (rng.standard_normal((D,), dtype=np.float32) / 64),
        "L0_attn_Wk": (rng.standard_normal((D, D), dtype=np.float32) / 64),
        "L0_attn_bk": (rng.standard_normal((D,), dtype=np.float32) / 64),
        "L0_attn_Wv": (rng.standard_normal((D, D), dtype=np.float32) / 64),
        "L0_attn_bv": (rng.standard_normal((D,), dtype=np.float32) / 64),
        "L0_attn_Wo": (rng.standard_normal((D, D), dtype=np.float32) / 64),
        "L0_attn_bo": (rng.standard_normal((D,), dtype=np.float32) / 64),
        "L0_ffn_A": (rng.standard_normal((NEX, LR, D), dtype=np.float32) / 64),
        "L0_ffn_B": (rng.standard_normal((NEX, D, LR), dtype=np.float32) * 0.02),
        "L0_ffn_Wr": (rng.standard_normal((NEX, D), dtype=np.float32) * 0.02),
    }
    out = kernel(**ins)
    print("out", out.shape, out.dtype, np.abs(out).mean())


# version 26
# speedup vs baseline: 1.0728x; 1.0428x over previous
import math
import numpy as np
from contextlib import ExitStack

import concourse.bass as bass
import concourse.mybir as mybir
from concourse.bass_utils import run_bass_kernel_spmd

F32 = mybir.dt.float32
BF16 = mybir.dt.bfloat16
AF = mybir.ActivationFunctionType
ALU = mybir.AluOpType

NCORES = 8
T = 2048
D = 4096
NCH = 32
NH = 32
HD = 128
NEX = 4
LR = 16
EPS = 1e-5
EXP_SCALE = 1.0 / math.sqrt(HD)
EXP_BIAS = -40.0
GATE_BIAS = -10.0

ENGS = ["gpsimd", "sync", "tensor", "scalar", "vector"]


class Track:
    def __init__(self, name):
        self.name = name
        self.n = 0
        self.h = None


class Buf:
    def __init__(self, name, rng=None):
        self.name = name
        self.w = None
        self.rd = {}
        self.dsem = None
        self.rng = rng


class Plan:
    def __init__(self, mk_sem):
        self.ranged = []
        self.ops = {e: [] for e in ENGS}
        self.etr = {e: Track("t_" + e) for e in ENGS}
        self.dsems = []
        self.mk_sem = mk_sem
        self.waited = {e: {} for e in ENGS}

    def all_tracks(self):
        return list(self.etr.values()) + self.dsems

    def get_dsem(self, buf):
        if buf.dsem is None:
            tr = Track("d_" + buf.name)
            tr.h = self.mk_sem(tr.name)
            buf.dsem = tr
            self.dsems.append(tr)
        return buf.dsem

    def _wait(self, eng, tr, n):
        if n <= 0 or self.waited[eng].get(tr.name, 0) >= n:
            return
        self.waited[eng][tr.name] = n
        self.ops[eng].append(lambda e, h=tr, m=n: e.wait_ge(h.h, m))

    def step(self, eng, fns, reads=(), writes=(), dma=False, dsem=None):
        for b in list(reads) + list(writes):
            if b.w is not None:
                self._wait(eng, b.w[0], b.w[1])
            if b.rng is not None:
                for ob in self.ranged:
                    if ob is b or ob.rng[1] <= b.rng[0] or ob.rng[0] >= b.rng[1]:
                        continue
                    if ob.w is not None:
                        self._wait(eng, ob.w[0], ob.w[1])
                    if b in writes:
                        for tr, n in ob.rd.values():
                            self._wait(eng, tr, n)
        for b in writes:
            for tr, n in b.rd.values():
                self._wait(eng, tr, n)
        if not isinstance(fns, (list, tuple)):
            fns = [fns]
        if dma:
            assert dsem is not None, "dma step needs dsem buf"
            tr = self.get_dsem(dsem)
        else:
            tr = self.etr[eng]
        amt = 16 if dma else 1
        tr.n += amt
        fns = list(fns)
        last = fns[-1]
        fns[-1] = lambda e, f=last, h=tr, a=amt: f(e).then_inc(h.h, a)
        self.ops[eng].extend(fns)
        cert = (tr, tr.n)
        for b in writes:
            b.w = cert
            b.rd = {}
        for b in reads:
            b.rd[tr.name] = cert

    def barrier(self):
        snap = [(t, t.n) for t in self.all_tracks()]
        for e in ENGS:
            for t, n in snap:
                if t is self.etr[e]:
                    continue
                self._wait(e, t, n)

    def emit(self, block):
        for name in ENGS:
            fns = self.ops[name]
            if not fns:
                continue

            def body(e, fns=fns):
                for f in fns:
                    f(e)

            getattr(block, name)(body)


def build_nc():
    nc = bass.Bass("TRN2", target_bir_lowering=False)

    xh = nc.dram_tensor("xh", [128, NCH, T], F32, kind="ExternalInput").ap()
    wqkv = nc.dram_tensor("wqkv", [NH, 3, 128, NCH, 128], F32, kind="ExternalInput").ap()
    wo_a = nc.dram_tensor("wo_a", [NCH, 128, NH, 128], F32, kind="ExternalInput").ap()
    a_a = nc.dram_tensor("a_a", [128, NCH, NEX * LR], F32, kind="ExternalInput").ap()
    b_a = nc.dram_tensor("b_a", [NEX * LR, D], F32, kind="ExternalInput").ap()
    wr_a = nc.dram_tensor("wr_a", [128, NCH, NEX], F32, kind="ExternalInput").ap()
    bias_a = nc.dram_tensor("bias_a", [NH, 3 * 128], F32, kind="ExternalInput").ap()
    bo_a = nc.dram_tensor("bo_a", [NCH, 128], F32, kind="ExternalInput").ap()
    eye_a = nc.dram_tensor("eye_a", [128, 128], F32, kind="ExternalInput").ap()
    mpad_a = nc.dram_tensor("mpad_a", [128, 1152], F32, kind="ExternalInput").ap()
    ones_a = nc.dram_tensor("ones_a", [1, 512], F32, kind="ExternalInput").ap()
    e_a = nc.dram_tensor("e_a", [NEX, NEX * LR], F32, kind="ExternalInput").ap()

    out_hbm = nc.dram_tensor("out_hbm", [128, NCH, T], F32, kind="ExternalOutput").ap()
    dbg_hbm = nc.dram_tensor("dbg_hbm", [128, NCH * T], BF16, kind="ExternalOutput").ap()
    av_hbm = nc.dram_tensor("av_hbm", [4, 128, NH, 512], BF16, kind="Internal").ap()
    wo_hbm = nc.dram_tensor("wo_hbm", [NCH, 128, NH, 128], BF16, kind="Internal").ap()

    st = ExitStack()
    E_ = st.enter_context

    RT = E_(nc.sbuf_tensor("RT", [128, NCH * T], BF16))
    BIG = E_(nc.sbuf_tensor("BIG", [128, 28672], BF16))
    Rb = E_(nc.sbuf_tensor("Rb", [128, T], BF16))
    rbsb = E_(nc.sbuf_tensor("rbsb", [128, 512], BF16))
    lnrow2 = E_(nc.sbuf_tensor("lnrow2", [2, T], BF16))
    mbf = lnrow2[0:1, :]
    stg = E_(nc.sbuf_tensor("stg", [1, 512], BF16))
    scr0 = E_(nc.sbuf_tensor("scr0", [1, 512], F32))
    scr1 = E_(nc.sbuf_tensor("scr1", [1, 512], F32))
    nrec = E_(nc.sbuf_tensor("nrec", [1, 512], BF16))
    srtmp = E_(nc.sbuf_tensor("srtmp", [1, 512], F32))
    augsb = [E_(nc.sbuf_tensor(f"augsb{i}", [2, 384], BF16)) for i in range(2)]
    augwr = E_(nc.sbuf_tensor("augwr", [1, NEX], BF16))
    auga = E_(nc.sbuf_tensor("auga", [1, NEX * LR], BF16))
    boj = [E_(nc.sbuf_tensor(f"boj{i}", [1, 128], BF16)) for i in range(2)]
    ident = E_(nc.sbuf_tensor("ident", [128, 128], BF16))
    mpad = E_(nc.sbuf_tensor("mpad", [128, 1152], BF16))
    ones_f = E_(nc.sbuf_tensor("ones_f", [1, 128], F32))
    ones_b = E_(nc.sbuf_tensor("ones_b", [1, 512], BF16))
    ones_c = E_(nc.sbuf_tensor("ones_c", [128, 1], BF16))
    e_mat = E_(nc.sbuf_tensor("e_mat", [NEX, NEX * LR], BF16))
    ct = E_(nc.sbuf_tensor("ct", [128, 8], F32))

    psA = E_(nc.psum_tensor("psA", [128, 1024], F32))
    psB = E_(nc.psum_tensor("psB", [128, 1024], F32))
    psC = E_(nc.psum_tensor("psC", [128, 1024], F32))
    psE0 = E_(nc.psum_tensor("psE0", [128, 512], F32))
    psE1 = E_(nc.psum_tensor("psE1", [128, 512], F32))
    psAb = psA[:, :].bitcast(BF16)
    psCb = psC[:, :].bitcast(BF16)

    p = Plan(lambda name: E_(nc.semaphore(name)))
    for tr in p.etr.values():
        tr.h = E_(nc.semaphore(tr.name))

    B = lambda n: Buf(n)
    rt = B("rt")
    rtg = [B(f"rtg{i}") for i in range(8)]

    def rts_bufs(*cs):
        out = [rtg[c // 4] for c in cs]
        return out
    bA, bB = B("psA"), B("psB")
    bClo, bChi = B("psC_lo"), B("psC_hi")
    bE0, bE1 = B("psE0"), B("psE1")
    bRb, bmbf, bstg = B("Rb"), B("lnrow2"), B("stg")
    bscr0, bscr1, bnrec, brb = B("scr0"), B("scr1"), B("nrec"), B("rbsb")
    bsrt = B("srtmp")
    baugs = [B("augsb0"), B("augsb1")]
    baugwr, bauga = B("augwr"), B("auga")
    bboj = [B("boj0"), B("boj1")]
    cstD, cstM = B("cstD"), B("cstM")
    bavhbm = [[B(f"av_hbm_{m}_{hg}") for hg in range(4)] for m in range(4)]
    bwohbm = B("wo_hbm")
    bwq, bwk, bwv = B("wq"), B("wk"), B("wv")
    bqT2 = [B("qT0"), B("qT1")]
    bkT2 = [B("kT0"), B("kT1")]
    bvH2 = [B("vH0"), B("vH1")]
    bvT = B("vT")
    bP = [B("P0"), B("P1")]
    bav = [B("av0"), B("av1")]
    basb, bwrsb = B("asb"), B("wrsb")
    bbsb = [B("bsb0"), B("bsb1")]
    bgmid, bgrb, blog, bgT = B("gmid"), B("grb"), B("logits"), B("gT")
    bgate = B("gatescr")
    bxsq = [B("xsq0"), B("xsq1")]
    boutst = [B("outst0"), B("outst1")]
    bavq, bwo = B("avq"), [B("wo0"), B("wo1")]
    bavqp = [B(f"avqp{i}") for i in range(4)]

    _ranges = {
        bwq: (0, 4096), bwk: (4096, 8192), bwv: (8192, 12288),
        bqT2[0]: (12288, 14336), bqT2[1]: (14336, 16384),
        bkT2[0]: (16384, 18432), bkT2[1]: (18432, 20480),
        bvH2[0]: (20480, 22528), bvH2[1]: (22528, 24576),
        bvT: (24576, 26624),
        bP[0]: (26624, 27136), bP[1]: (27136, 27648),
        bav[0]: (27648, 28160), bav[1]: (28160, 28672),
        bavqp[0]: (0, 4096), bavqp[1]: (4096, 8192),
        bavqp[2]: (8192, 12288), bavqp[3]: (12288, 16384),
        bwo[0]: (16384, 20480), bwo[1]: (20480, 24576),
        basb: (0, 2048), bwrsb: (2048, 2176),
        bbsb[0]: (2176, 3200), bbsb[1]: (3200, 4224),
        bgmid: (4224, 6272), bgrb: (6272, 8320),
        blog: (8320, 10368), bgT: (10368, 12416),
        bgate: (12416, 13056),
        bxsq[0]: (26624, 27136), bxsq[1]: (27136, 27648),
        boutst[0]: (14080, 16128), boutst[1]: (16128, 18176),
    }
    for _b, _r in _ranges.items():
        _b.rng = _r
        p.ranged.append(_b)
    bavqp2 = [B("avqp0a"), B("avqp0b")]

    wq_sb = BIG[:, 0:4096]
    wk_sb = BIG[:, 4096:8192]
    wv_sb = BIG[:, 8192:12288]
    qT2 = [BIG[:, 12288:14336], BIG[:, 14336:16384]]
    kT2 = [BIG[:, 16384:18432], BIG[:, 18432:20480]]
    vH2 = [BIG[:, 20480:22528], BIG[:, 22528:24576]]
    vT = BIG[:, 24576:26624]
    Psb = [BIG[:, 26624:27136], BIG[:, 27136:27648]]
    avsb = [BIG[:, 27648:28160], BIG[:, 28160:28672]]
    avq = BIG[:, 0:16384]
    wosb = [BIG[:, 16384:20480], BIG[:, 20480:24576]]
    a_sb = BIG[:, 0:2048]
    wr_sb = BIG[:, 2048:2176]
    b_sb = [BIG[0:64, 2176:3200], BIG[0:64, 3200:4224]]
    gmid = BIG[0:64, 4224:6272]
    grb = BIG[0:64, 6272:8320]
    logits = BIG[0:4, 8320:10368]
    gT_t = BIG[0:4, 10368:12416]
    Ltok = BIG[:, 12416:12480]
    c1t = BIG[:, 12480:12544]
    c2t = BIG[:, 12544:12608]
    c3t = BIG[:, 12608:12672]
    Gbf = BIG[:, 12672:12736]
    gf32 = BIG[:, 12736:12864].bitcast(F32)
    gf32b = BIG[:, 12864:12992].bitcast(F32)
    zt = BIG[:, 12992:13024].bitcast(F32)
    invt = BIG[:, 13024:13056].bitcast(F32)
    xsq = [BIG[:, 26624:27136], BIG[:, 27136:27648]]
    outst = [BIG[:, 14080 + 2048 * i: 14080 + 2048 * (i + 1)].bitcast(F32) for i in range(2)]

    ct_eps = ct[0:1, 0:1]
    ct_nh = ct[0:1, 1:2]
    ct_ph = ct[0:1, 2:3]
    ct_es = ct[:, 3:4]
    ct_eb = ct[:, 4:5]
    ct_gb = ct[:, 5:6]
    ct_m1 = ct[0:1, 6:7]

    RTs = lambda c, o, w: RT[:, T * c + o: T * c + o + w]


    def vmemsets(v):
        v.memset(ones_c[:, :], 1.0)
        v.memset(ct[:, 0:1], EPS)
        v.memset(ct[:, 1:2], -0.5)
        v.memset(ct[:, 2:3], 0.5)
        v.memset(ct[:, 3:4], EXP_SCALE)
        v.memset(ct[:, 4:5], EXP_BIAS)
        v.memset(ct[:, 5:6], GATE_BIAS)
        v.memset(Psb[0], 0.0)
        v.memset(Psb[1], 0.0)
        return v.memset(ct[:, 6:7], -1.0)

    p.step("vector", vmemsets, writes=[cstM, bP[0], bP[1]])

    for i in range(4):
        p.step("gpsimd",
               lambda g, i=i: g.dma_start(
                   RT[:, 16384 * i: 16384 * (i + 1)],
                   xh[:, 8 * i: 8 * (i + 1), :].rearrange("p c t -> p (c t)")),
               writes=[rt, rtg[2 * i], rtg[2 * i + 1]], dma=True, dsem=rtg[2 * i])
    p.step("gpsimd", lambda g: g.dma_start(ident[:, :], eye_a), writes=[cstD], dma=True, dsem=cstD)
    p.step("gpsimd", lambda g: g.dma_start(mpad[:, :], mpad_a), writes=[cstD], dma=True, dsem=cstD)
    p.step("gpsimd", lambda g: g.dma_start(ones_f[:, :], ones_a[:, 0:128]), writes=[cstD], dma=True, dsem=cstD)
    p.step("gpsimd", lambda g: g.dma_start(ones_b[:, :], ones_a), writes=[cstD], dma=True, dsem=cstD)
    p.step("gpsimd", lambda g: g.dma_start(e_mat[:, :], e_a), writes=[cstD], dma=True, dsem=cstD)
    p.step("gpsimd", lambda g: g.dma_start(augsb[0][1:2, :], bias_a[0:1, :]), writes=[baugs[0]], dma=True, dsem=baugs[0])
    p.step("gpsimd", lambda g: g.dma_start(augsb[1][1:2, :], bias_a[1:2, :]), writes=[baugs[1]], dma=True, dsem=baugs[1])

    def plan_stats_tq(tq):
        import os as _os
        nch = int(_os.environ.get("KMK_NCH", "32"))
        fin = int(_os.environ.get("KMK_FIN", "9"))
        if True:
            o = 512 * tq
            for c in range(nch):
                par = c % 2
                p.step("vector",
                       lambda v, c=c, o=o, par=par: v.tensor_tensor(
                           xsq[par], RTs(c, o, 512), RTs(c, o, 512), ALU.mult),
                       reads=rts_bufs(c), writes=[bxsq[par], bP[par]])

                def mms(t, c=c, o=o, par=par):
                    t.matmul(psC[0:1, 0:512], ones_c[:, :], RTs(c, o, 512),
                             start=(c == 0), stop=(c == nch - 1))
                    return t.matmul(psC[0:1, 512:1024], ones_c[:, :], xsq[par],
                                    start=(c == 0), stop=(c == nch - 1))
                p.step("tensor", mms, reads=rts_bufs(c) + [bxsq[par], cstM],
                       writes=[bClo, bChi])
            if not fin:
                return
            if fin >= 1:
                p.step("vector",
                       lambda v: v.tensor_scalar_mul(scr0[0:1, 0:512], psC[0:1, 0:512], 1.0 / D),
                       reads=[bClo], writes=[bscr0])
            if fin >= 2:
                p.step("vector",
                       lambda v, o=o: v.tensor_copy(mbf[0:1, o:o + 512], scr0[0:1, 0:512]),
                       reads=[bscr0], writes=[bmbf])
            if fin >= 3:
                p.step("vector",
                       lambda v: v.tensor_tensor(scr1[0:1, 0:512], scr0[0:1, 0:512],
                                                 scr0[0:1, 0:512], ALU.mult),
                       reads=[bscr0], writes=[bscr1])
            if fin >= 4:
                p.step("vector",
                       lambda v: v.scalar_tensor_tensor(scr0[0:1, 0:512], psC[0:1, 512:1024],
                                                        1.0 / D, scr1[0:1, 0:512],
                                                        ALU.mult, ALU.subtract),
                       reads=[bChi, bscr1], writes=[bscr0])
            if fin >= 5:
                p.step("scalar",
                       lambda s: s.activation(scr1[0:1, 0:512], scr0[0:1, 0:512], AF.Ln,
                                              bias=ct_eps),
                       reads=[bscr0, cstM], writes=[bscr1])
            if fin >= 6:
                p.step("scalar",
                       lambda s: s.activation(scr0[0:1, 0:512], scr1[0:1, 0:512], AF.Exp,
                                              scale=ct_nh),
                       reads=[bscr1, cstM], writes=[bscr0])
            if fin >= 7:
                p.step("scalar",
                       lambda s: s.activation(stg[0:1, 0:512], scr1[0:1, 0:512],
                                              AF.Exp, scale=ct_ph),
                       reads=[bscr1, cstM], writes=[bstg])
                p.step("gpsimd",
                       lambda g, o=o: g.dma_start(lnrow2[1:2, o:o + 512], stg[0:1, 0:512]),
                       reads=[bstg, bmbf], writes=[bmbf], dma=True, dsem=bmbf)
            if fin >= 8:
                p.step("tensor",
                       lambda t: t.matmul(psE0[:, :], ones_f[:, :], scr0[0:1, 0:512],
                                          start=True, stop=True),
                       reads=[bscr0, cstD], writes=[bE0])
            if fin >= 9:
                p.step("vector",
                       lambda v, o=o: v.tensor_copy(Rb[:, o:o + 512], psE0[:, :]),
                       reads=[bE0], writes=[bRb])

    def plan_stats_couter():
        sumv = [psA[0:1, 0:512], psA[0:1, 512:1024], psB[0:1, 0:512], psB[0:1, 512:1024]]
        ssqv = [psC[0:1, 0:512], psC[0:1, 512:1024], psE0[0:1, :], psE1[0:1, :]]
        sumb = [bA, bA, bB, bB]
        ssqb = [bClo, bChi, bE0, bE1]
        for c in range(NCH):
            for tq in range(4):
                o = 512 * tq
                par = (4 * c + tq) % 2
                p.step("vector",
                       lambda v, c=c, o=o, par=par: v.tensor_tensor(
                           xsq[par], RTs(c, o, 512), RTs(c, o, 512), ALU.mult),
                       reads=rts_bufs(c), writes=[bxsq[par], bP[par]])

                def mms(t, c=c, o=o, par=par, tq=tq):
                    t.matmul(sumv[tq], ones_c[:, :], RTs(c, o, 512),
                             start=(c == 0), stop=(c == NCH - 1))
                    return t.matmul(ssqv[tq], ones_c[:, :], xsq[par],
                                    start=(c == 0), stop=(c == NCH - 1))
                p.step("tensor", mms, reads=rts_bufs(c) + [bxsq[par], cstM],
                       writes=[sumb[tq], ssqb[tq]])
        for tq in range(4):
            o = 512 * tq
            p.step("vector",
                   lambda v, tq=tq: v.tensor_scalar_mul(scr0[0:1, 0:512], sumv[tq], 1.0 / D),
                   reads=[sumb[tq]], writes=[bscr0])
            p.step("vector",
                   lambda v, o=o: v.tensor_copy(mbf[0:1, o:o + 512], scr0[0:1, 0:512]),
                   reads=[bscr0], writes=[bmbf])
            p.step("vector",
                   lambda v: v.tensor_tensor(scr1[0:1, 0:512], scr0[0:1, 0:512],
                                             scr0[0:1, 0:512], ALU.mult),
                   reads=[bscr0], writes=[bscr1])
            p.step("vector",
                   lambda v, tq=tq: v.scalar_tensor_tensor(scr0[0:1, 0:512], ssqv[tq],
                                                           1.0 / D, scr1[0:1, 0:512],
                                                           ALU.mult, ALU.subtract),
                   reads=[ssqb[tq], bscr1], writes=[bscr0])
            p.step("scalar",
                   lambda s: s.activation(scr1[0:1, 0:512], scr0[0:1, 0:512], AF.Ln,
                                          bias=ct_eps),
                   reads=[bscr0, cstM], writes=[bscr1])
            p.step("scalar",
                   lambda s: s.activation(scr0[0:1, 0:512], scr1[0:1, 0:512], AF.Exp,
                                          scale=ct_nh),
                   reads=[bscr1, cstM], writes=[bscr0])
            p.step("scalar",
                   lambda s: s.activation(stg[0:1, 0:512], scr1[0:1, 0:512],
                                          AF.Exp, scale=ct_ph),
                   reads=[bscr1, cstM], writes=[bstg])
            p.step("gpsimd",
                   lambda g, o=o: g.dma_start(lnrow2[1:2, o:o + 512], stg[0:1, 0:512]),
                   reads=[bstg, bmbf], writes=[bmbf], dma=True, dsem=bmbf)
            rbv = [psC[:, 0:512], psC[:, 512:1024], psE0[:, :], psE1[:, :]][tq]
            rbb = [bClo, bChi, bE0, bE1][tq]
            p.step("tensor",
                   lambda t, rbv=rbv: t.matmul(rbv, ones_f[:, :], scr0[0:1, 0:512],
                                               start=True, stop=True),
                   reads=[bscr0, cstD], writes=[rbb])
            p.step("vector",
                   lambda v, o=o, rbv=rbv: v.tensor_copy(Rb[:, o:o + 512], rbv),
                   reads=[rbb], writes=[bRb])

    def plan_ffn(final, sub="all", skip_stats=False):
        order = ["dma", "stats", "spre", "router", "mid", "gates", "delta"]
        lim = order.index(sub) if sub in order else 99

        def upto(k):
            return order.index(k) <= lim
        p.step("gpsimd", lambda g: g.dma_start(a_sb, a_a.rearrange("p c j -> p (c j)")),
               writes=[basb], dma=True, dsem=basb)
        p.step("gpsimd", lambda g: g.dma_start(wr_sb, wr_a.rearrange("p c j -> p (c j)")),
               writes=[bwrsb], dma=True, dsem=bwrsb)
        p.step("gpsimd", lambda g: g.dma_start(b_sb[0], b_a[:, 0:1024]),
               writes=[bbsb[0]], dma=True, dsem=bbsb[0])

        if not upto("stats"):
            return
        if not skip_stats:
            if not final:
                plan_stats_couter()
            else:
                for _tq in range(4):
                    plan_stats_tq(_tq)
        if not upto("spre"):
            return

        def swr(t):
            mm = None
            for c in range(NCH):
                mm = t.matmul(psA[0:1, 0:NEX], ones_c[:, :], wr_sb[:, NEX * c:NEX * (c + 1)],
                              start=(c == 0), stop=(c == NCH - 1))
            return mm
        p.step("tensor", swr, reads=[bwrsb, cstM], writes=[bA])

        def sa(t):
            mm = None
            for c in range(NCH):
                mm = t.matmul(psB[0:1, 0:64], ones_c[:, :], a_sb[:, 64 * c:64 * (c + 1)],
                              start=(c == 0), stop=(c == NCH - 1))
            return mm
        p.step("tensor", sa, reads=[basb, cstM], writes=[bB])
        p.step("vector", lambda v: v.tensor_scalar_mul(augwr[:, :], psA[0:1, 0:NEX], -1.0),
               reads=[bA], writes=[baugwr])
        p.step("vector", lambda v: v.tensor_scalar_mul(auga[:, :], psB[0:1, 0:64], -1.0),
               reads=[bB], writes=[bauga])

        if not upto("router"):
            return
        for tq in range(4):
            o = 512 * tq

            def rmm(t, o=o):
                for c in range(NCH):
                    t.matmul(psE0[0:4, :], wr_sb[:, NEX * c:NEX * (c + 1)],
                             RTs(c, o, 512), start=(c == 0), stop=False)
                return t.matmul(psE0[0:4, :], augwr[:, :], mbf[0:1, o:o + 512],
                                start=False, stop=True)
            p.step("tensor", rmm, reads=[rt, bwrsb, baugwr, bmbf] + rtg, writes=[bE0])
            p.step("vector",
                   lambda v, o=o: v.tensor_tensor(logits[:, o:o + 512], psE0[0:4, :],
                                                  Rb[0:4, o:o + 512], ALU.mult),
                   reads=[bE0, bRb], writes=[blog])

        if not upto("mid"):
            return
        if not upto("gates"):
            return
        def ltr(t):
            mm = None
            for i in range(16):
                mm = t.transpose(psCb[:, 4 * i:4 * (i + 1)], logits[:, 128 * i:128 * (i + 1)],
                                 ident[0:4, 0:4])
            return mm
        p.step("tensor", ltr, reads=[blog, cstD], writes=[bClo, bChi])
        p.step("vector", lambda v: v.tensor_copy(Ltok, psCb[:, 0:64]),
               reads=[bClo, bChi], writes=[bgate])

        for half, (ps, bps) in enumerate([(psA, bA), (psB, bB)]):
            ho = 1024 * half

            def midmm(t, ho=ho, ps=ps):
                for c in range(NCH):
                    t.matmul(ps[0:64, 0:512], a_sb[:, 64 * c:64 * (c + 1)],
                             RTs(c, ho, 512), start=(c == 0), stop=False)
                    t.matmul(ps[0:64, 512:1024], a_sb[:, 64 * c:64 * (c + 1)],
                             RTs(c, ho + 512, 512), start=(c == 0), stop=False)
                t.matmul(ps[0:64, 0:512], auga[:, :], mbf[0:1, ho:ho + 512],
                         start=False, stop=True)
                return t.matmul(ps[0:64, 512:1024], auga[:, :], mbf[0:1, ho + 512:ho + 1024],
                                start=False, stop=True)
            p.step("tensor", midmm, reads=[rt, basb, bauga, bmbf] + rtg, writes=[bps])

        lv = lambda ap, e: ap.rearrange("p (i e) -> p i e", e=4)[:, :, e]
        gv = lambda e: gf32b.rearrange("p (i e) -> p i e", e=4)[:, :, e]

        def cmps(v):
            last = None
            for e in range(4):
                tgts = [c1t, c2t, c3t]
                k = 0
                for j in range(4):
                    if j == e:
                        continue
                    op = ALU.is_ge if j < e else ALU.is_gt
                    last = v.tensor_tensor(lv(tgts[k], e), lv(Ltok, j), lv(Ltok, e), op)
                    k += 1
            return last
        p.step("vector", cmps, reads=[bgate], writes=[bgate])
        p.step("vector", lambda v: v.tensor_tensor(c1t, c1t, c2t, ALU.add),
               reads=[bgate], writes=[bgate])
        p.step("vector", lambda v: v.tensor_tensor(c1t, c1t, c3t, ALU.add),
               reads=[bgate], writes=[bgate])
        p.step("vector", lambda v: v.tensor_scalar(c2t, c1t, 1.5, None, ALU.is_le),
               reads=[bgate], writes=[bgate])
        p.step("scalar",
               lambda s: s.activation(gf32[:, :], Ltok, AF.Exp, bias=ct_gb),
               reads=[bgate, cstM], writes=[bgate])
        p.step("vector", lambda v: v.tensor_tensor(gf32b[:, :], gf32[:, :], c2t, ALU.mult),
               reads=[bgate], writes=[bgate])
        p.step("vector", lambda v: v.tensor_tensor(zt[:, :], gv(0), gv(1), ALU.add),
               reads=[bgate], writes=[bgate])
        p.step("vector", lambda v: v.tensor_tensor(zt[:, :], zt[:, :], gv(2), ALU.add),
               reads=[bgate], writes=[bgate])
        p.step("vector", lambda v: v.tensor_tensor(zt[:, :], zt[:, :], gv(3), ALU.add),
               reads=[bgate], writes=[bgate])
        p.step("vector", lambda v: v.reciprocal(invt[:, :], zt[:, :]),
               reads=[bgate], writes=[bgate])

        def gmuls(v):
            last = None
            for e in range(4):
                last = v.tensor_tensor(lv(Gbf, e), gv(e), invt[:, :], ALU.mult)
            return last
        p.step("vector", gmuls, reads=[bgate], writes=[bgate])

        def gtr(t):
            mm = None
            for i in range(16):
                mm = t.transpose(psCb[0:4, 128 * i:128 * (i + 1)], Gbf[:, 4 * i:4 * (i + 1)],
                                 ident[:, :])
            return mm
        p.step("tensor", gtr, reads=[bgate, cstD], writes=[bClo, bChi])
        p.step("vector", lambda v: v.tensor_copy(gT_t[:, 0:1024], psCb[0:4, 0:1024]),
               reads=[bClo, bChi], writes=[bgT])
        p.step("vector", lambda v: v.tensor_copy(gT_t[:, 1024:2048], psCb[0:4, 1024:2048]),
               reads=[bClo, bChi], writes=[bgT])

        for half, bmid in enumerate([bA, bB]):
            ho = 1024 * half
            psM = psA if half == 0 else psB

            def emm(t, ho=ho):
                t.matmul(psC[0:64, 0:512], e_mat[:, :], gT_t[:, ho:ho + 512],
                         start=True, stop=True)
                return t.matmul(psC[0:64, 512:1024], e_mat[:, :], gT_t[:, ho + 512:ho + 1024],
                                start=True, stop=True)
            p.step("tensor", emm, reads=[bgT, cstD], writes=[bClo, bChi])
            p.step("vector",
                   lambda v, ho=ho: v.tensor_tensor(grb[:, ho:ho + 1024], psC[0:64, 0:1024],
                                                    Rb[0:64, ho:ho + 1024], ALU.mult),
                   reads=[bClo, bChi, bRb], writes=[bgrb])
            p.step("vector",
                   lambda v, ho=ho, psM=psM: v.tensor_tensor(
                       gmid[:, ho:ho + 1024], psM[0:64, 0:1024], grb[:, ho:ho + 1024],
                       ALU.mult),
                   reads=[bmid, bgrb], writes=[bgmid])

        if not upto("delta"):
            return
        for g in range(4):
            if g < 3:
                p.step("gpsimd",
                       lambda gg, g=g: gg.dma_start(b_sb[(g + 1) % 2],
                                                    b_a[:, 1024 * (g + 1):1024 * (g + 2)]),
                       writes=[bbsb[(g + 1) % 2]], dma=True, dsem=bbsb[(g + 1) % 2])
            for j in range(8 * g, 8 * g + 8):
                jo = 128 * (j % 8)
                alt = j % 2 == 1
                if not alt:
                    slots = [(psA, 0), (psA, 512), (psB, 0), (psB, 512)]
                    wbufs = [bA, bB]
                else:
                    slots = [(psC, 0), (psC, 512), (psE0, 0), (psE1, 0)]
                    wbufs = [bClo, bChi, bE0, bE1]

                def dmm(t, jo=jo, g=g, slots=slots):
                    bs = b_sb[g % 2]
                    mm = None
                    for q, (ps, off) in enumerate(slots):
                        mm = t.matmul(ps[:, off:off + 512], bs[:, jo:jo + 128],
                                      gmid[:, 512 * q:512 * (q + 1)],
                                      start=True, stop=True)
                    return mm
                p.step("tensor", dmm, reads=[bbsb[g % 2], bgmid], writes=wbufs)
                if not final:
                    if not alt:
                        p.step("vector",
                               lambda v, j=j: v.tensor_tensor(RTs(j, 0, 1024),
                                                              RTs(j, 0, 1024),
                                                              psA[:, :], ALU.add),
                               reads=[bA], writes=[rt] + rts_bufs(j))
                        p.step("vector",
                               lambda v, j=j: v.tensor_tensor(RTs(j, 1024, 1024),
                                                              RTs(j, 1024, 1024),
                                                              psB[:, :], ALU.add),
                               reads=[bB], writes=[rt] + rts_bufs(j))
                    else:
                        p.step("vector",
                               lambda v, j=j: v.tensor_tensor(RTs(j, 0, 1024),
                                                              RTs(j, 0, 1024),
                                                              psC[:, :], ALU.add),
                               reads=[bClo, bChi], writes=[rt] + rts_bufs(j))
                        p.step("vector",
                               lambda v, j=j: v.tensor_tensor(RTs(j, 1024, 512),
                                                              RTs(j, 1024, 512),
                                                              psE0[:, :], ALU.add),
                               reads=[bE0], writes=[rt] + rts_bufs(j))
                        p.step("vector",
                               lambda v, j=j: v.tensor_tensor(RTs(j, 1536, 512),
                                                              RTs(j, 1536, 512),
                                                              psE1[:, :], ALU.add),
                               reads=[bE1], writes=[rt] + rts_bufs(j))
                else:
                    par0, par1 = j % 2, (j + 1) % 2
                    if not alt:
                        p.step("vector",
                               lambda v, j=j, par=par0: v.tensor_tensor(
                                   outst[par][:, 0:1024], RTs(j, 0, 1024),
                                   psA[:, :], ALU.add),
                               reads=[bA, rt] + rts_bufs(j), writes=[boutst[par0]])
                        p.step("vector",
                               lambda v, j=j, par=par1: v.tensor_tensor(
                                   outst[par][:, 0:1024], RTs(j, 1024, 1024),
                                   psB[:, :], ALU.add),
                               reads=[bB, rt] + rts_bufs(j), writes=[boutst[par1]])
                    else:
                        p.step("vector",
                               lambda v, j=j, par=par0: v.tensor_tensor(
                                   outst[par][:, 0:1024], RTs(j, 0, 1024),
                                   psC[:, :], ALU.add),
                               reads=[bClo, bChi, rt] + rts_bufs(j), writes=[boutst[par0]])
                        p.step("vector",
                               lambda v, j=j, par=par1: v.tensor_tensor(
                                   outst[par][:, 0:512], RTs(j, 1024, 512),
                                   psE0[:, :], ALU.add),
                               reads=[bE0, rt] + rts_bufs(j), writes=[boutst[par1]])
                        p.step("vector",
                               lambda v, j=j, par=par1: v.tensor_tensor(
                                   outst[par][:, 512:1024], RTs(j, 1536, 512),
                                   psE1[:, :], ALU.add),
                               reads=[bE1, rt] + rts_bufs(j), writes=[boutst[par1]])
                    for half, par in ((0, par0), (1, par1)):
                        p.step("sync",
                               lambda y, j=j, half=half, par=par: y.dma_start(
                                   out_hbm[:, j, 1024 * half:1024 * (half + 1)],
                                   outst[par][:, 0:1024]),
                               reads=[boutst[par]], dma=True, dsem=boutst[par])

    def plan_head(h):
        par_h = h % 2
        qT = qT2[par_h]
        kT = kT2[par_h]
        vH = vH2[par_h]
        bqT, bkT, bvH = bqT2[par_h], bkT2[par_h], bvH2[par_h]
        proj_t = []
        projs = [
            ("q", wq_sb, bwq, qT, bqT, 0),
            ("k", wk_sb, bwk, kT, bkT, 128),
            ("v", wv_sb, bwv, vT, bvT, 256),
        ]
        for pi, (nm, wsb, bw, oT, boT, so) in enumerate(projs):
            ppar = pi % 2
            def srow(t, wsb=wsb):
                mm = None
                for g in range(8):
                    mm = t.matmul(psC[0:1, 512:1024], ones_c[:, :],
                                  wsb[:, 512 * g:512 * (g + 1)],
                                  start=(g == 0), stop=(g == 7))
                return mm
            proj_t.append(lambda srow=srow, bw=bw: p.step(
                "tensor", srow, reads=[bw, cstM], writes=[bChi]))
            proj_t.append(lambda: p.step(
                "vector",
                lambda v: v.tensor_copy(srtmp[0:1, 0:512], psC[0:1, 512:1024]),
                reads=[bChi], writes=[bsrt]))
            proj_t.append(lambda: p.step(
                "vector",
                lambda v: v.tensor_tensor(srtmp[0:1, 0:256], srtmp[0:1, 0:256],
                                          srtmp[0:1, 256:512], ALU.add),
                reads=[bsrt], writes=[bsrt]))
            proj_t.append(lambda so=so: p.step(
                "vector",
                lambda v: v.scalar_tensor_tensor(augsb[par_h][0:1, so:so + 128],
                                                 srtmp[0:1, 0:128], -1.0,
                                                 srtmp[0:1, 128:256],
                                                 ALU.mult, ALU.subtract),
                reads=[bsrt], writes=[baugs[par_h]]))
            for c in range(NCH):
                def cmm(t, c=c, wsb=wsb):
                    wch = wsb[:, 128 * c:128 * (c + 1)]
                    t.matmul(psA[:, 0:512], wch, RTs(c, 0, 512), start=(c == 0), stop=False)
                    t.matmul(psA[:, 512:1024], wch, RTs(c, 512, 512), start=(c == 0), stop=False)
                    t.matmul(psB[:, 0:512], wch, RTs(c, 1024, 512), start=(c == 0), stop=False)
                    return t.matmul(psB[:, 512:1024], wch, RTs(c, 1536, 512),
                                    start=(c == 0), stop=False)
                proj_t.append(lambda c=c, cmm=cmm, bw=bw: p.step(
                    "tensor", cmm, reads=rts_bufs(c) + [rt, bw], writes=[bA, bB]))
            if h + 1 < NH:
                proj_t.append(lambda pi=pi, wsb=wsb, bw=bw: p.step(
                    "gpsimd",
                    lambda g: g.dma_start(wsb, wqkv[h + 1, pi].rearrange("p c j -> p (c j)")),
                    writes=[bw], dma=True, dsem=bw))

            def augmm(t, so=so):
                ab = augsb[par_h]
                mm = None
                for half, ps in enumerate([psA, psB]):
                    ho = 1024 * half
                    t.matmul(ps[:, 0:512], ab[0:2, so:so + 128], lnrow2[0:2, ho:ho + 512],
                             start=False, stop=True)
                    mm = t.matmul(ps[:, 512:1024], ab[0:2, so:so + 128],
                                  lnrow2[0:2, ho + 512:ho + 1024], start=False, stop=True)
                return mm
            proj_t.append(lambda augmm=augmm: p.step(
                "tensor", augmm, reads=[baugs[par_h], bmbf], writes=[bA, bB]))
            for half, (ps, bp) in enumerate([(psA, bA), (psB, bB)]):
                ho = 1024 * half
                proj_t.append(lambda ps=ps, bp=bp, ho=ho, oT=oT, boT=boT: p.step(
                    "vector",
                    lambda v: v.tensor_tensor(oT[:, ho:ho + 1024], ps[:, :],
                                              Rb[:, ho:ho + 1024], ALU.mult),
                    reads=[bp, bRb], writes=[boT]))
        if h + 1 < NH:
            proj_t.append(lambda: p.step(
                "gpsimd",
                lambda g: g.dma_start(augsb[(h + 1) % 2][1:2, :], bias_a[h + 1:h + 2, :]),
                writes=[baugs[(h + 1) % 2]], dma=True, dsem=baugs[(h + 1) % 2]))
        proj_t.append(lambda: p.step(
            "gpsimd",
            lambda g: g.dma_start(wo_hbm[h].rearrange("b c d -> b (c d)"),
                                  wo_a[h].rearrange("b c d -> b (c d)")),
            writes=[bwohbm], dma=True, dsem=bwohbm))

        def vtr(t):
            mm = None
            for i in range(16):
                mm = t.transpose(psAb[:, 128 * i:128 * (i + 1)],
                                 vT[:, 128 * i:128 * (i + 1)], ident[:, :])
            return mm
        proj_t.append(lambda: p.step("tensor", vtr, reads=[bvT, cstD], writes=[bA]))
        proj_t.append(lambda vH=vH, bvH=bvH: p.step(
            "vector", lambda v: v.tensor_copy(vH[:, :], psAb[:, :]),
            reads=[bA], writes=[bvH]))

        attn_t = []

        def smm_step(m4, c, qT=qT, kT=kT, bqT=bqT, bkT=bkT):
            j0 = max(0, c - 4 * m4)
            w = 512 - 128 * j0

            def smm(t, m4=m4, c=c, w=w):
                return t.matmul(psC[:, 0:w], kT[:, 128 * c:128 * (c + 1)],
                                qT[:, 512 * m4 + 512 - w: 512 * (m4 + 1)],
                                start=True, stop=True)
            p.step("tensor", smm, reads=[bkT, bqT], writes=[bClo])

        for m4 in range(4):
            nch = 4 * (m4 + 1)
            if m4 == 0:
                attn_t.append(lambda: smm_step(0, 0))
            for c in range(nch):
                par = c % 2
                j0 = max(0, c - 4 * m4)
                rs = 128 * j0
                diag = c >= 4 * m4

                w = 512 - rs

                def expst(par=par, w=w, rs=rs):
                    p.step("scalar",
                           lambda s: s.activation(Psb[par][:, rs:512], psC[:, 0:w],
                                                  AF.Exp, bias=ct_eb, scale=ct_es),
                           reads=[bClo, cstM], writes=[bP[par]])
                attn_t.append(expst)
                if diag:
                    mw = rs + 128

                    def maskst(par=par, mw=mw):
                        p.step("vector",
                               lambda v: v.tensor_tensor(Psb[par][:, 0:mw],
                                                         Psb[par][:, 0:mw],
                                                         mpad[:, 1152 - mw:1152], ALU.mult),
                               reads=[cstD], writes=[bP[par]])
                    attn_t.append(maskst)
                if c + 1 < nch:
                    attn_t.append(lambda m4=m4, c=c: smm_step(m4, c + 1))

                def savst(c=c, par=par, m4=m4, nch=nch, vH=vH, bvH=bvH):
                    def sav(t):
                        return t.matmul(psE0[:, :], vH[:, 128 * c:128 * (c + 1)],
                                        Psb[par][:, :], start=(c == 0), stop=(c == nch - 1))
                    p.step("tensor", sav, reads=[bP[par], bvH], writes=[bE0])
                attn_t.append(savst)
                if c % 2 == 1 or c == nch - 1:
                    cpair = [c - 1, c] if c % 2 == 1 else [c]

                    def sumst(cpair=cpair, nch=nch):
                        def summ(t):
                            mm = None
                            for cc in cpair:
                                mm = t.matmul(psE1[0:1, :], ones_c[:, :], Psb[cc % 2][:, :],
                                              start=(cc == 0), stop=(cc == nch - 1))
                            return mm
                        p.step("tensor", summ,
                               reads=[bP[cc % 2] for cc in cpair] + [cstM], writes=[bE1])
                    attn_t.append(sumst)

            if m4 < 3:
                attn_t.append(lambda m4=m4: smm_step(m4 + 1, 0))

            def tailst(m4=m4):
                p.step("scalar",
                       lambda s: s.activation(scr0[0:1, 0:512], psE1[0:1, :], AF.Ln),
                       reads=[bE1], writes=[bscr0])
                p.step("scalar",
                       lambda s: s.activation(nrec[0:1, 0:512], scr0[0:1, 0:512], AF.Exp,
                                              scale=ct_m1),
                       reads=[bscr0, cstM], writes=[bnrec])
                p.step("tensor",
                       lambda t: t.matmul(psE1[:, :], ones_b[0:1, 0:128], nrec[0:1, 0:512],
                                          start=True, stop=True),
                       reads=[bnrec, cstD], writes=[bE1])
                p.step("vector", lambda v: v.tensor_copy(rbsb[:, :], psE1[:, :]),
                       reads=[bE1], writes=[brb])
                par = (h * 4 + m4) % 2
                p.step("vector",
                       lambda v, par=par: v.tensor_tensor(avsb[par][:, :], psE0[:, :],
                                                          rbsb[:, :], ALU.mult),
                       reads=[bE0, brb], writes=[bav[par]])
                p.step("sync",
                       lambda y, par=par: y.dma_start(av_hbm[m4, :, h, :], avsb[par][:, :]),
                       reads=[bav[par]], writes=[bavhbm[m4][h // 8]],
                       dma=True, dsem=bav[par])
            attn_t.append(tailst)
        return proj_t, attn_t

    def emit_interleaved(a, b):
        if not b:
            for f in a:
                f()
            return
        ratio = len(a) / len(b)
        bi = 0.0
        j = 0
        for i, f in enumerate(a):
            f()
            while (i + 1) / ratio >= j + 1 and j < len(b):
                b[j]()
                j += 1
        while j < len(b):
            b[j]()
            j += 1

    def plan_oproj():
        def load_avq_piece(tq, pc):
            p.step("gpsimd",
                   lambda g, tq=tq, pc=pc: g.dma_start(
                       avq[:, 4096 * pc:4096 * (pc + 1)],
                       av_hbm[tq, :, 8 * pc:8 * (pc + 1), :].rearrange("p h j -> p (h j)")),
                   reads=[bavhbm[tq][pc]], writes=[bavqp[pc]],
                   dma=True, dsem=bavqp[pc])

        for tq in range(4):
            for pc in range(4):
                load_avq_piece(tq, pc)
            if tq == 0:
                p.step("gpsimd",
                       lambda g: g.dma_start(
                           wosb[0], wo_hbm[0].rearrange("p h j -> p (h j)")),
                       reads=[bwohbm], writes=[bwo[0]], dma=True, dsem=bwo[0])
                p.step("gpsimd", lambda g: g.dma_start(boj[0][:, :], bo_a[0:1, :]),
                       writes=[bboj[0]], dma=True, dsem=bboj[0])
            js = list(range(NCH)) if tq % 2 == 0 else list(range(NCH - 1, -1, -1))
            for ji, j in enumerate(js):
                par = j % 2
                ps = psA if par == 0 else psB
                bp = bA if par == 0 else bB
                if ji + 1 < NCH:
                    jn = js[ji + 1]
                    p.step("gpsimd",
                           lambda g, jn=jn: g.dma_start(
                               wosb[jn % 2], wo_hbm[jn].rearrange("p h j -> p (h j)")),
                           reads=[bwohbm], writes=[bwo[jn % 2]], dma=True, dsem=bwo[jn % 2])
                    p.step("gpsimd",
                           lambda g, jn=jn: g.dma_start(boj[jn % 2][:, :],
                                                        bo_a[jn:jn + 1, :]),
                           writes=[bboj[jn % 2]], dma=True, dsem=bboj[jn % 2])

                for pc in range(4):
                    def ommp(t, j=j, par=par, ps=ps, pc=pc):
                        w = wosb[par]
                        mm = None
                        for hh in range(8 * pc, 8 * (pc + 1)):
                            mm = t.matmul(ps[:, 0:512], w[:, 128 * hh:128 * (hh + 1)],
                                          avq[:, 512 * hh:512 * (hh + 1)],
                                          start=(hh == 0), stop=False)
                        return mm
                    p.step("tensor", ommp, reads=[bavqp[pc], bwo[par], cstD], writes=[bp])
                p.step("tensor",
                       lambda t, par=par, ps=ps: t.matmul(ps[:, 0:512], boj[par][:, :],
                                                          ones_b[:, :], start=False, stop=True),
                       reads=[bboj[par], cstD], writes=[bp])
                p.step("vector",
                       lambda v, j=j, tq=tq, ps=ps: v.tensor_tensor(
                           RTs(j, 512 * tq, 512), RTs(j, 512 * tq, 512), ps[:, 0:512],
                           ALU.add),
                       reads=[bp], writes=[rt] + rts_bufs(j))
            plan_stats_tq(tq)

    def dump_rt():
        p.step("sync", lambda y: y.dma_start(dbg_hbm, RT[:, :]), reads=[rt],
               dma=True, dsem=rt)

    import os
    stage = int(os.environ.get("KMK_STAGE", "6"))
    sub = os.environ.get("KMK_SUB", "all")
    dbg = int(os.environ.get("KMK_DBG", "0"))
    plan_ffn(final=False, sub=sub)
    if dbg == 1:
        dump_rt()
    if stage >= 2:
        for _tq in range(4):
            plan_stats_tq(_tq)
    if stage >= 3:
        for pi, (wsb, wb) in enumerate([(wq_sb, bwq), (wk_sb, bwk), (wv_sb, bwv)]):
            p.step("gpsimd",
                   lambda g, pi=pi, wsb=wsb: g.dma_start(
                       wsb, wqkv[0, pi].rearrange("p c j -> p (c j)")),
                   writes=[wb], dma=True, dsem=wb)
        nh = 1 if stage == 3 else NH
        prev_attn = []
        for h in range(nh):
            proj_t, attn_t = plan_head(h)
            emit_interleaved(proj_t, prev_attn)
            prev_attn = attn_t
        for f in prev_attn:
            f()
    if stage >= 5:
        plan_oproj()
        if dbg == 2:
            dump_rt()
    if stage >= 6:
        plan_ffn(final=True, skip_stats=True)

    for tr in p.dsems:
        p.ops["sync"].append(lambda e, h=tr, n=tr.n: e.wait_ge(h.h, n))

    block = E_(nc.Block())
    p.emit(block)
    st.close()
    return nc


def _prep(inputs):
    f = lambda k: np.asarray(inputs[k], np.float32)
    x = f("L0_attn_x")
    Wq, Wk, Wv, Wo = f("L0_attn_Wq"), f("L0_attn_Wk"), f("L0_attn_Wv"), f("L0_attn_Wo")
    bq, bk, bv, bo = f("L0_attn_bq"), f("L0_attn_bk"), f("L0_attn_bv"), f("L0_attn_bo")
    A, Bm, Wr = f("L0_ffn_A"), f("L0_ffn_B"), f("L0_ffn_Wr")

    def wblocks(W):
        WT = np.ascontiguousarray(W.T)
        return np.ascontiguousarray(
            WT.reshape(NCH, 128, NH, 128).transpose(2, 1, 0, 3))

    wq_b, wk_b, wv_b = wblocks(Wq), wblocks(Wk), wblocks(Wv)
    wqkv = np.ascontiguousarray(np.stack([wq_b, wk_b, wv_b], axis=1))
    WoT = np.ascontiguousarray(Wo.T)
    wo_a = np.ascontiguousarray(
        WoT.reshape(NH, 128, NCH, 128).transpose(2, 1, 0, 3))
    a_a = np.ascontiguousarray(
        A.transpose(2, 0, 1).reshape(D, NEX * LR).reshape(NCH, 128, NEX * LR)
        .transpose(1, 0, 2))
    b_a = np.ascontiguousarray(Bm.transpose(0, 2, 1).reshape(NEX * LR, D))
    wr_a = np.ascontiguousarray(
        Wr.T.reshape(NCH, 128, NEX).transpose(1, 0, 2))
    bias_a = np.ascontiguousarray(
        np.stack([bq.reshape(NH, 128), bk.reshape(NH, 128), bv.reshape(NH, 128)],
                 axis=1).reshape(NH, 3 * 128))
    bo_a = np.ascontiguousarray(bo.reshape(NCH, 128))
    eye_a = np.eye(128, dtype=np.float32)
    kk = np.arange(128)[:, None]
    col = np.arange(1152)[None, :]
    mpad_a = ((col - 1024) >= kk).astype(np.float32)
    ones_a = np.ones((1, 512), np.float32)
    e_a = np.zeros((NEX, NEX * LR), np.float32)
    for e in range(NEX):
        e_a[e, LR * e:LR * (e + 1)] = 2.0

    shared = dict(wqkv=wqkv, wo_a=wo_a, a_a=a_a, b_a=b_a, wr_a=wr_a,
                  bias_a=bias_a, bo_a=bo_a, eye_a=eye_a, mpad_a=mpad_a,
                  ones_a=ones_a, e_a=e_a)
    in_maps = []
    for b in range(NCORES):
        xh = np.ascontiguousarray(
            x[b].T.reshape(NCH, 128, T).transpose(1, 0, 2))
        m = dict(shared)
        m["xh"] = xh
        in_maps.append(m)
    return in_maps


def kernel(**inputs):
    in_maps = _prep(inputs)
    nc = build_nc()
    res = run_bass_kernel_spmd(nc, in_maps, core_ids=list(range(NCORES)))
    outs = []
    for r in res.results:
        o = r["out_hbm"]
        outs.append(np.ascontiguousarray(o.transpose(1, 0, 2).reshape(D, T).T))
    return np.stack(outs).astype(np.float32)


if __name__ == "__main__":
    rng = np.random.default_rng(0)
    ins = {
        "L0_attn_x": rng.standard_normal((8, T, D), dtype=np.float32),
        "L0_attn_Wq": (rng.standard_normal((D, D), dtype=np.float32) / 64),
        "L0_attn_bq": (rng.standard_normal((D,), dtype=np.float32) / 64),
        "L0_attn_Wk": (rng.standard_normal((D, D), dtype=np.float32) / 64),
        "L0_attn_bk": (rng.standard_normal((D,), dtype=np.float32) / 64),
        "L0_attn_Wv": (rng.standard_normal((D, D), dtype=np.float32) / 64),
        "L0_attn_bv": (rng.standard_normal((D,), dtype=np.float32) / 64),
        "L0_attn_Wo": (rng.standard_normal((D, D), dtype=np.float32) / 64),
        "L0_attn_bo": (rng.standard_normal((D,), dtype=np.float32) / 64),
        "L0_ffn_A": (rng.standard_normal((NEX, LR, D), dtype=np.float32) / 64),
        "L0_ffn_B": (rng.standard_normal((NEX, D, LR), dtype=np.float32) * 0.02),
        "L0_ffn_Wr": (rng.standard_normal((NEX, D), dtype=np.float32) * 0.02),
    }
    out = kernel(**ins)
    print("out", out.shape, out.dtype, np.abs(out).mean())
